# version 02 (final)
# baseline (speedup 1.0000x reference)
import contextlib
import numpy as np
import ml_dtypes
import concourse.bass as bass
import concourse.mybir as mybir
from concourse.bass_utils import run_bass_kernel_spmd

F32 = mybir.dt.float32
BF16 = mybir.dt.bfloat16
LP16 = mybir.dt.float16
I32 = mybir.dt.int32
AF = mybir.ActivationFunctionType
ALU = mybir.AluOpType
AX = mybir.AxisListType

BF = ml_dtypes.bfloat16
NP16 = np.float16
DEBUG_STREAMS = False
_DBG = {}

D = 2048
T = 2048
NCHUNK = 16
NTCH = 16
POLL_CAP = 20_000_000
LN_EPS = 1e-5


class Prog:
    def __init__(self, name="prog"):
        self.nc = bass.Bass("TRN2", target_bir_lowering=False, debug=False)
        self.ops = []
        self.st = contextlib.ExitStack()
        self._uid = 0

    def uid(self, tag):
        self._uid += 1
        return f"{tag}_{self._uid}"

    def sbuf(self, tag, shape, dt):
        return self.st.enter_context(self.nc.sbuf_tensor(self.uid(tag), shape, dt))

    def psum(self, tag, shape, dt=F32):
        return self.st.enter_context(self.nc.psum_tensor(self.uid(tag), shape, dt))

    def dram(self, tag, shape, dt, addr_space="Local"):
        return self.nc.dram_tensor(self.uid(tag), shape, dt, kind="Internal",
                                   addr_space=addr_space).ap()

    def inp(self, name, shape, dt):
        return self.nc.dram_tensor(name, shape, dt, kind="ExternalInput").ap()

    def out(self, name, shape, dt):
        return self.nc.dram_tensor(name, shape, dt, kind="ExternalOutput").ap()

    def op(self, eng, fn, inc=1):
        import traceback
        frames = traceback.extract_stack()[:-2]
        site = ">".join(str(fr.lineno) for fr in frames if fr.filename.endswith("kernel.py"))
        self.ops.append((eng, inc, fn, site))

    def dma(self, out, in_, eng="sync", allow_slow=False):
        self.op(eng, lambda e, o=out, i=in_, a=allow_slow:
                e.dma_start(out=o, in_=i, allow_slow_non_contiguous=a), 16)

    def dma_cast(self, out, in_):
        self.op("gpsimd", lambda e, o=out, i=in_: e.dma_start(out=o, in_=i), 16)

    def mm(self, out, lhsT, rhs, start, stop):
        self.op("tensor", lambda e, o=out, l=lhsT, r=rhs, s=start, p=stop:
                e.matmul(o, l, r, start=s, stop=p))

    def transp(self, out, in_, ident):
        self.op("tensor", lambda e, o=out, i=in_, d=ident: e.transpose(o, i, d))

    def act(self, out, in_, func, bias=None, scale=None, accum_out=None):
        def fn(e, o=out, i=in_, f=func, b=bias, s=scale, a=accum_out):
            kw = {}
            if b is not None:
                kw["bias"] = b
            if s is not None:
                kw["scale"] = s
            if a is not None:
                kw["accum_out"] = a
            return e.activation(out=o, in_=i, func=f, **kw)
        self.op("scalar", fn)

    def tt(self, out, in0, in1, alu, eng="vector"):
        self.op(eng, lambda e, o=out, a=in0, b=in1, op=alu:
                e.tensor_tensor(out=o, in0=a, in1=b, op=op))

    def ts(self, out, in0, s1, op0, s2=None, op1=None, eng="vector"):
        if s2 is None:
            s2, op1 = 0.0, ALU.bypass
        self.op(eng, lambda e, o=out, i=in0, a=s1, oa=op0, b=s2, ob=op1:
                e.tensor_scalar(out=o, in0=i, scalar1=a, scalar2=b, op0=oa, op1=ob))

    def red(self, out, in_, alu, eng="vector"):
        self.op(eng, lambda e, o=out, i=in_, op=alu:
                e.tensor_reduce(out=o, in_=i, axis=AX.X, op=op))

    def recip(self, out, in_):
        self.op("vector", lambda e, o=out, i=in_: e.reciprocal(out=o, in_=i))

    def copy(self, out, in_, eng="vector"):
        if eng == "scalar":
            self.act(out, in_, AF.Copy)
        else:
            self.op(eng, lambda e, o=out, i=in_: e.tensor_copy(out=o, in_=i))

    def memset(self, ap, v, eng="vector"):
        self.op(eng, lambda e, a=ap, vv=v: e.memset(a, vv))

    def poll(self, flag_ap, nonce_sb_ap, eng="sync"):
        tag = self.uid("poll")

        def fn(e, fa=flag_ap, na=nonce_sb_ap, t=tag):
            with contextlib.ExitStack() as st:
                fr = st.enter_context(e.register(f"fr_{t}"))
                nr = st.enter_context(e.register(f"nr_{t}"))
                cr = st.enter_context(e.register(f"cr_{t}"))
                gt = st.enter_context(e.register(f"gt_{t}"))
                bud = st.enter_context(e.register(f"bud_{t}"))
                e.load(nr, na)
                e.reg_mov(bud, POLL_CAP)
                e.load(fr, fa)
                e.reg_alu(cr, fr, nr, ALU.not_equal)
                with e.While(cr):
                    e.reg_sub(bud, bud, 1)
                    e.load(fr, fa)
                    e.reg_alu(cr, fr, nr, ALU.not_equal)
                    e.reg_alu(gt, bud, 0, ALU.is_gt)
                    e.reg_alu(cr, cr, gt, ALU.bitwise_and)
            return e.nop()
        self.op(eng, fn)

    def pdma(self, out_even, out_odd, in_even, in_odd, eng="sync"):
        def fn(e, oe=out_even, oo=out_odd, ie=in_even, io=in_odd):
            par = self._parreg[e]
            with e.If_eq(par, 0):
                e.dma_start(out=oe, in_=ie).then_inc(self._gsem, 16)
            with e.Else():
                e.dma_start(out=oo, in_=io).then_inc(self._gsem, 16)
            return None
        self.ops.append((eng, 16, ("raw", fn), "pdma"))

    def finalize(self, limit=None):
        if limit is not None:
            self.ops = self.ops[:limit]
        nc = self.nc
        sem = self.st.enter_context(nc.semaphore("gsem"))
        self._gsem = sem
        block = self.st.enter_context(nc.Block())

        sched = []
        c = 0
        i = 0
        ops = self.ops
        while i < len(ops):
            eng, inc, fn, site = ops[i]
            if eng == "tensor" and not isinstance(fn, tuple):
                j = i
                run = []
                tot = 0
                while j < len(ops) and ops[j][0] == "tensor" and not isinstance(ops[j][2], tuple):
                    run.append(ops[j])
                    tot += ops[j][1]
                    j += 1
                for k, (eg, ic, f, st_) in enumerate(run):
                    w = c if k == 0 else None
                    ii = tot if k == len(run) - 1 else 0
                    sched.append((eg, f, w, ii, st_))
                c += tot
                i = j
            else:
                sched.append((eng, fn, c, inc, site))
                c += inc
                i += 1
        total = c
        by_eng = {}
        for eng, fn, wait, inc, site in sched:
            by_eng.setdefault(eng, []).append((fn, wait, inc, site))

        self._parreg = {}

        def emit_for(engobj, items):
            pr = engobj.register(self.uid("parity")).__enter__()
            pid = engobj.partition_id()
            engobj.reg_alu(pr, pid, 1, ALU.bitwise_and)
            self._parreg[engobj] = pr
            for fn, wait, inc, site in items:
                if wait is not None and wait > 0:
                    engobj.wait_ge(sem, wait)
                try:
                    if isinstance(fn, tuple) and fn[0] == "raw":
                        fn[1](engobj)
                    else:
                        instr = fn(engobj)
                        if inc:
                            instr.then_inc(sem, inc)
                except Exception as ex:
                    raise RuntimeError(f"op emitted at {site} failed: {ex}") from ex

        @block.sync
        def _(s):
            emit_for(s, by_eng.get("sync", []))
            s.wait_ge(sem, total)

        @block.gpsimd
        def _(g):
            emit_for(g, by_eng.get("gpsimd", []))

        @block.tensor
        def _(t):
            emit_for(t, by_eng.get("tensor", []))

        @block.vector
        def _(v):
            emit_for(v, by_eng.get("vector", []))

        @block.scalar
        def _(s):
            emit_for(s, by_eng.get("scalar", []))

        self.st.close()
        return nc


R_ONES128 = 0
SC_EPS = 128
SC_INV2048 = 129
SC_NEG1E30 = 130
SC_LAEPS = 131
SC_ONE = 132
SC_L2EPS = 133
SC_EPS24 = 134
SC_SQN = 135
CROW_COLS = 256
C_ONES = 0


def host_consts():
    consts = np.zeros((128, 16), np.float32)
    consts[:, C_ONES] = 1.0
    consts[:, 1] = 1e-24
    crow = np.zeros((1, CROW_COLS), np.float32)
    crow[0, 0:128] = 1.0
    crow[0, SC_EPS] = LN_EPS
    crow[0, SC_INV2048] = 1.0 / 2048.0
    crow[0, SC_NEG1E30] = -1e30
    crow[0, SC_LAEPS] = 1e-8
    crow[0, SC_ONE] = 1.0
    crow[0, SC_L2EPS] = 1e-12
    crow[0, SC_EPS24] = 1e-24
    crow[0, SC_SQN] = 256.0 / 2048.0
    return consts, crow


class Common:
    def __init__(self, p: Prog):
        self.p = p
        self.consts_d = p.inp("consts", [128, 16], F32)
        self.crow_d = p.inp("crow", [1, CROW_COLS], F32)
        self.ident_d = p.inp("ident_bf", [128, 128], LP16)
        self.identf_d = p.inp("ident_f32", [128, 128], F32)
        self.nonce_d = p.inp("nonce", [1, 4], I32)

        self.consts = p.sbuf("consts", [128, 16], F32)
        self.crow = p.sbuf("crow", [1, CROW_COLS], F32)
        self.onesb = p.sbuf("onesb", [1, 128], LP16)
        self.negonesb = p.sbuf("negonesb", [1, 128], LP16)
        self.onescol_b = p.sbuf("onescol_b", [128, 1], LP16)
        self.onescol_bb = p.sbuf("onescol_bb", [128, 1], BF16)
        self.ident = p.sbuf("ident", [128, 128], LP16)
        self.identf = p.sbuf("identf", [128, 128], F32)
        self.nonce = p.sbuf("nonce", [1, 4], I32)

        p.dma(self.consts[:], self.consts_d)
        p.dma(self.crow[:], self.crow_d)
        p.dma(self.ident[:], self.ident_d)
        p.dma(self.identf[:], self.identf_d)
        p.dma(self.nonce[:], self.nonce_d)
        p.nc.const_aps.register(F32, 1e-24, self.consts[:, 1:2])
        p.copy(self.onesb[:], self.crow[0:1, 0:128])
        p.ts(self.negonesb[:], self.crow[0:1, 0:128], -1.0, ALU.mult)
        p.copy(self.onescol_b[:], self.consts[:, C_ONES:C_ONES + 1])
        p.copy(self.onescol_bb[:], self.consts[:, C_ONES:C_ONES + 1])

    def onesf(self, n=128):
        return self.crow[0:1, 0:n]

    def onescol(self):
        return self.consts[:, C_ONES:C_ONES + 1]

    def sc(self, col):
        return self.crow[0:1, col:col + 1]


class Pools:

    def __init__(self, p: Prog):
        self.xn = p.sbuf("xn_pool", [128, NCHUNK * T], LP16)
        self.w = p.sbuf("w_pool", [128, 34816], LP16)
        self.ab = p.sbuf("a_pool", [128, 16384], LP16)
        self.f = p.sbuf("f_pool", [128, 3072], F32)
        self.rows = p.sbuf("rows_pool", [128, 4096], F32)
        self.pp0 = p.psum("pp0", [128, 2048])
        self.pp1 = p.psum("pp1", [128, 2048])

    def row(self, base, off, n):
        return self.rows[base:base + 1, off:off + n]


def emit_layernorm(p: Prog, cm: Common, pl: Pools, x_d, xn_dram=None):
    xc = pl.f[:, 0:1024]
    xw = pl.f[:, 1024:1536]
    tmp = pl.f[:, 1536:2048]
    xb = pl.ab[:, 0:1024]
    x2 = pl.ab[:, 1024:2048]
    sums = pl.pp0[0:1, 0:1024]
    sumsq = pl.pp0[0:1, 1024:2048]
    bc = pl.pp1[:, 0:1024]
    mrow = pl.row(0, 0, T)
    rrow = pl.row(0, 2048, T)
    srow = pl.row(32, 0, T)
    s2row = pl.row(32, 2048, T)

    for h in range(2):
        hs = slice(h * 1024, (h + 1) * 1024)
        for c in range(NCHUNK):
            p.dma(xc, x_d[c * 128:(c + 1) * 128, hs])
            p.copy(xb, xc)
            p.act(x2, xc, AF.Square, scale=0.0625)
            for n in range(2):
                cs = slice(n * 512, (n + 1) * 512)
                p.mm(sums[0:1, cs], cm.onescol_b[:], xb[:, cs], start=(c == 0), stop=(c == NCHUNK - 1))
                p.mm(sumsq[0:1, cs], cm.onescol_b[:], x2[:, cs], start=(c == 0), stop=(c == NCHUNK - 1))
        mr = mrow[0:1, hs]
        rr = rrow[0:1, hs]
        sr = srow[0:1, hs]
        s2r = s2row[0:1, hs]
        p.act(mr, sums, AF.Copy, scale=cm.sc(SC_INV2048))
        p.act(sr, sumsq, AF.Copy, scale=cm.sc(SC_SQN))
        p.tt(s2r, mr, mr, ALU.mult)
        p.tt(sr, sr, s2r, ALU.subtract)
        p.act(s2r, sr, AF.Ln, bias=cm.sc(SC_EPS))
        p.act(rr, s2r, AF.Exp, scale=-0.5)
        for n in range(2):
            gn = h * 1024 + n * 512
            cs = slice(gn, gn + 512)
            p.mm(bc[:, 0:512], cm.onesf(), mrow[0:1, cs], start=True, stop=True)
            p.mm(bc[:, 512:1024], cm.onesf(), rrow[0:1, cs], start=True, stop=True)
            for c in range(NCHUNK):
                p.dma(xw, x_d[c * 128:(c + 1) * 128, cs])
                p.tt(tmp, xw, bc[:, 0:512], ALU.subtract)
                if xn_dram is None:
                    p.tt(pl.xn[:, c * T + gn: c * T + gn + 512], tmp, bc[:, 512:1024], ALU.mult)
                else:
                    p.tt(xb[:, 0:512], tmp, bc[:, 512:1024], ALU.mult)
                    p.dma(xn_dram[c * 128:(c + 1) * 128, cs], xb[:, 0:512])


def load_w(p, w_d, dst, cin, ocols):
    p.dma_cast(dst, w_d[cin * 128:(cin + 1) * 128, ocols])


def emit_l0ffn(p: Prog, cm: Common, pl: Pools, x_d, xout_d, wd):
    wsb = pl.w[:, 0:32768]
    asb = pl.w[:, 32768:34816]
    bsb = pl.ab[:, 1152:3200]
    gsb = pl.ab[:, 0:128]
    tall = pl.ab[:, 128:640]
    tsc = pl.ab[:, 640:1152]
    xres = pl.f[:, 0:512]
    osb = pl.f[:, 512:1024]
    e16f = pl.f[0:8, 1024:1152]
    cw_f = pl.rows[0:8, 0:2048]

    for c in range(NCHUNK):
        load_w(p, wd["wT"], wsb[:, c * D:(c + 1) * D], c, slice(0, D))
        load_w(p, wd["aT"], asb[:, c * 128:(c + 1) * 128], c, slice(0, 128))
        load_w(p, wd["gwT"], gsb[:, c * 8:(c + 1) * 8], c, slice(0, 8))
    p.dma_cast(bsb[:, 0:D], wd["bT"])
    p.dma(e16f, wd["e16"])

    scps = pl.pp0[:, 0:8]
    trps = pl.pp1[0:8, 0:128]
    m1 = pl.f[:, 2048:2049]
    nm1 = pl.f[:, 2049:2050]
    m2 = pl.f[:, 2050:2051]
    ll = pl.f[:, 2051:2052]
    rl = pl.f[:, 2052:2053]
    ex = pl.f[:, 2056:2064]
    msk = pl.f[:, 2064:2072]
    s2 = pl.f[:, 2072:2080]
    ww = pl.f[:, 2080:2088]
    comb = pl.f[:, 2088:2096]

    for tc in range(NTCH):
        tcs = slice(tc * 128, (tc + 1) * 128)
        for c in range(NCHUNK):
            p.mm(scps, pl.xn[:, c * T + tc * 128: c * T + (tc + 1) * 128],
                 gsb[:, c * 8:(c + 1) * 8], start=(c == 0), stop=(c == NCHUNK - 1))
        p.red(m1, scps, ALU.max)
        p.ts(nm1, m1, -1.0, ALU.mult)
        p.act(ex, scps, AF.Exp, bias=nm1)
        p.ts(msk, scps, m1, ALU.is_ge)
        p.ts(s2, msk, -1e30, ALU.mult)
        p.tt(s2, scps, s2, ALU.add)
        p.red(m2, s2, ALU.max)
        p.ts(msk, scps, m2, ALU.is_ge)
        p.tt(ww, ex, msk, ALU.mult)
        p.red(ll, ww, ALU.add)
        p.recip(rl, ll)
        p.ts(comb, ww, rl, ALU.mult)
        p.mm(trps, comb, cm.identf[:], start=True, stop=True)
        p.copy(cw_f[:, tcs], trps)

    tps = pl.pp0[:, 512:1024]
    cwx = pl.pp1[:, 512:1024]
    ops_ = pl.pp0[:, 1024:1536]
    for n in range(4):
        cs = slice(n * 512, (n + 1) * 512)
        for c in range(NCHUNK):
            p.mm(tps, asb[:, c * 128:(c + 1) * 128],
                 pl.xn[:, c * T + n * 512: c * T + (n + 1) * 512],
                 start=(c == 0), stop=(c == NCHUNK - 1))
        p.copy(tall, tps, eng="scalar")
        p.mm(cwx, e16f, cw_f[:, cs], start=True, stop=True)
        p.tt(tsc, tall, cwx, ALU.mult)
        for oc in range(NCHUNK):
            for c in range(NCHUNK):
                p.mm(ops_, wsb[:, c * D + oc * 128: c * D + (oc + 1) * 128],
                     pl.xn[:, c * T + n * 512: c * T + (n + 1) * 512],
                     start=(c == 0), stop=False)
            p.mm(ops_, bsb[:, oc * 128:(oc + 1) * 128], tsc, start=False, stop=True)
            p.dma(xres, x_d[oc * 128:(oc + 1) * 128, cs])
            p.tt(osb, ops_, xres, ALU.add)
            p.dma(xout_d[oc * 128:(oc + 1) * 128, cs], osb)


def _f(a):
    return np.ascontiguousarray(np.asarray(a, np.float32))


def prep_l0ffn(inputs):
    w = {}
    w["wT"] = _f(np.asarray(inputs["L0_ffn_weight"]).T)
    w["gwT"] = _f(np.asarray(inputs["L0_ffn_gate_w"]).T)
    A = _f(inputs["L0_ffn_A_w"])
    w["aT"] = _f(A.transpose(2, 0, 1).reshape(D, 128))
    Bw = _f(inputs["L0_ffn_B_w"])
    w["bT"] = _f(Bw.transpose(0, 2, 1).reshape(128, D))
    e16 = np.zeros((8, 128), np.float32)
    for e in range(8):
        e16[e, 16 * e:16 * (e + 1)] = 16.0
    w["e16"] = e16
    return w


def host_ident():
    return np.eye(128, dtype=np.float32).astype(NP16), np.eye(128, dtype=np.float32)


def shard_tokens(x_b):
    blocks = np.asarray(x_b, np.float32).reshape(32, 128, D)
    ev = blocks[0::2].reshape(T, D)
    od = blocks[1::2].reshape(T, D)
    return _f(ev.T), _f(od.T)


def unshard_tokens(ev_T, od_T):
    ev = np.asarray(ev_T).T.reshape(16, 128, D)
    od = np.asarray(od_T).T.reshape(16, 128, D)
    out = np.empty((32, 128, D), np.float32)
    out[0::2] = ev
    out[1::2] = od
    return out.reshape(4096, D)



class Shr:
    FL_HELLO = 0
    FL_L0AA = 2
    FL_L0AB = 4
    FL_L1AA = 6
    FL_L1AB = 8

    def __init__(self, p: Prog):
        self.p = p
        self.flags = p.dram("shr_flags", [1, 64], I32, addr_space="Shared")
        self.zz = p.sbuf("shr_zz", [1, 4], I32)
        p.memset(self.zz[:], 0)

    def hello(self, cm):
        p = self.p
        for i in range(2, 16, 2):
            p.pdma(self.flags[0:1, i:i + 1], self.flags[0:1, i + 1:i + 2],
                   self.zz[0:1, 0:1], self.zz[0:1, 0:1])
        p.pdma(self.flags[0:1, 0:1], self.flags[0:1, 1:2],
               cm.nonce[0:1, 0:1], cm.nonce[0:1, 0:1])
        p.poll(self.flags[0:1, 0:1], cm.nonce[0:1, 0:1])
        p.poll(self.flags[0:1, 1:2], cm.nonce[0:1, 0:1])

    def set_flag(self, cm, idx):
        self.p.pdma(self.flags[0:1, idx:idx + 1], self.flags[0:1, idx + 1:idx + 2],
                    cm.nonce[0:1, 0:1], cm.nonce[0:1, 0:1])

    def wait_flag(self, cm, idx):
        self.p.poll(self.flags[0:1, idx:idx + 1], cm.nonce[0:1, 0:1])
        self.p.poll(self.flags[0:1, idx + 1:idx + 2], cm.nonce[0:1, 0:1])


def emit_attn_core(p: Prog, cm: Common, pl: Pools, tag, qdr, xattn, ksh, vsh,
                   nheads, hd, att_scale, causal, mode,
                   negC_bf=None, amask_sb=None, pv_bf=False):
    aug = (hd == 64)
    vtw = hd + 1 if aug else hd
    ksb = pl.ab[:, 0:4096]
    vsb = pl.ab[:, 4096:4096 + 2 * 16 * vtw]
    pb1 = pl.ab[:, 8448:8960]
    pb2 = pl.ab[:, 8960:9472]
    if pv_bf:
        vsb = vsb.bitcast(BF16)
        pb1 = pb1.bitcast(BF16)
        pb2 = pb2.bitcast(BF16)
        if amask_sb is not None:
            amask_sb = amask_sb.bitcast(BF16)
    vsbv = vsb.rearrange("p (t c) -> p t c", c=vtw)
    onescol_pv = cm.onescol_bb[:] if pv_bf else cm.onescol_b[:]
    qsb = pl.ab[:, 9472:9984]
    frow = pl.ab[0:1, 9984:10496]
    mbf = pl.ab[:, 10496:10497]
    osb = pl.f[:, 0:512]
    rl = pl.row(0, 512, 512)
    macc = pl.f[:, 513:514]
    mt = pl.f[:, 514:515]
    ones512 = pl.row(0, 0, 512)

    for ql in range(4):
        i_max = 4 * ql + 3
        nkb = (i_max + 1) if causal else 16
        for h in range(nheads):
            hc = (h * hd) // 128
            ho = (h * hd) % 128
            qh = qsb[ho:ho + hd, 0:512]
            p.dma(qh, qdr[h * hd:(h + 1) * hd, ql * 512:(ql + 1) * 512])
            for s in range(2):
                kr = s * nheads * hd + h * hd
                p.dma(ksb[ho:ho + hd, s * 2048: s * 2048 + nkb * 128],
                      ksh[kr:kr + hd, 0:nkb * 128])
            for s in range(2):
                dst = vsbv[:, s * 16:s * 16 + nkb, 0:hd]
                src = vsh[s * 2048: s * 2048 + nkb * 128, h * hd:(h + 1) * hd] \
                    .rearrange("(kb p) c -> p kb c", p=128)
                p.dma(dst, src)
                if aug:
                    p.memset(vsbv[:, s * 16:s * 16 + nkb, hd:hd + 1], 1.0)
            if mode == "bound":
                p.ts(frow, ones512, negC_bf[0:1, h:h + 1], ALU.mult)
            else:
                tps = pl.pp0[0:1, 1536:2048]
                for qb in range(4):
                    qhb = qsb[ho:ho + hd, qb * 128:(qb + 1) * 128]
                    first = True
                    for s in range(2):
                        for jg in range(0, nkb, 4):
                            w = min(4, nkb - jg) * 128
                            S1 = pl.pp0[:, 1024:1536]
                            p.mm(S1[:, 0:w], qhb,
                                 ksb[ho:ho + hd, s * 2048 + jg * 128: s * 2048 + jg * 128 + w],
                                 start=True, stop=True)
                            p.red(mt, S1[:, 0:w], ALU.max)
                            if first:
                                p.copy(macc, mt)
                                first = False
                            else:
                                p.tt(macc, macc, mt, ALU.max)
                    p.ts(mbf, macc, -1.0, ALU.mult)
                    p.mm(tps[0:1, qb * 128:(qb + 1) * 128], mbf, cm.ident[:],
                         start=True, stop=True)
                p.copy(frow, tps, eng="scalar")
            if aug:
                O = pl.pp1[0:65, 0:512]
            else:
                O = pl.pp1[0:128, 0:512]
                lps = pl.pp1[0:1, 1536:2048]
            first = True
            for s in range(2):
                for j in range(nkb):
                    Spp = pl.pp0[:, 0:512] if (j % 2 == 0) else pl.pp0[:, 512:1024]
                    p.mm(Spp, ksb[ho:ho + hd, s * 2048 + j * 128: s * 2048 + (j + 1) * 128],
                         qh, start=True, stop=False)
                    p.mm(Spp, cm.onesb[:], frow, start=False, stop=True)
                    p.act(pb1, Spp, AF.Exp, scale=att_scale)
                    pv_in = pb1
                    if causal and j >= i_max - 3:
                        o = j - (i_max - 3)
                        p.tt(pb2, pb1, amask_sb[:, (o * 2 + s) * 512:(o * 2 + s + 1) * 512],
                             ALU.mult)
                        pv_in = pb2
                    p.mm(O, vsb[:, (s * 16 + j) * vtw:(s * 16 + j) * vtw + vtw], pv_in,
                         start=first, stop=(aug and s == 1 and j == nkb - 1))
                    if not aug:
                        p.mm(lps, onescol_pv, pv_in,
                             start=first, stop=(s == 1 and j == nkb - 1))
                    first = False
            if aug:
                p.recip(rl, O[64:65, :])
                R = pl.pp1[0:64, 512:1024]
                p.mm(R, cm.onesf(64), rl, start=True, stop=True)
                p.copy(osb[0:64, :], O[0:64, :], eng="scalar")
                p.tt(xattn[ho:ho + 64, hc * T + ql * 512: hc * T + (ql + 1) * 512],
                     osb[0:64, :], R, ALU.mult)
            else:
                p.recip(rl, lps)
                R = pl.pp1[0:128, 512:1024]
                p.mm(R, cm.onesf(), rl, start=True, stop=True)
                p.copy(osb[0:128, :], O[0:128, :], eng="scalar")
                p.tt(xattn[:, hc * T + ql * 512: hc * T + (ql + 1) * 512],
                     osb[0:128, :], R, ALU.mult)


def emit_l0_attn_a(p: Prog, cm: Common, pl: Pools, shr: Shr, x_d, xout_d, wd):
    tag = "l0aa"
    amask_sb = pl.ab[:, 10752:14848]
    p.dma(amask_sb, wd["amask"])

    w2 = pl.w[:, 0:16384]
    w1t = pl.w[:, 16384:18432]
    qdr = p.dram(tag + "_q", [1024, 2048], LP16)
    xattn = pl.xn[:, 0:16384]
    xP = pl.ab[:, 0:8192]
    ev = pl.ab[:, 8192:8704]
    sqb = pl.ab[:, 8704:9216]
    browb = pl.ab[0:1, 9216:10240]
    biases = pl.f[:, 2100:2116]
    b2c = pl.f[:, 2116:2124]
    s1ps = pl.pp0[:, 0:512]
    s2ps = pl.pp0[:, 512:1024]
    ssps = pl.pp1[0:1, 0:1024]

    ksh = p.dram(tag + "_ksh", [2 * 1024, 2048], LP16, addr_space="Shared")
    vsh = p.dram(tag + "_vsh", [2 * 2048, 1024], LP16, addr_space="Shared")
    stsh = p.dram(tag + "_st", [2, 16], F32, addr_space="Shared")

    kssmax = pl.row(0, 1024, 16)
    qssmax = pl.row(0, 1040, 16)
    tmps = pl.row(0, 1056, 16)
    tmp1 = pl.row(0, 1072, 1)
    negC = pl.row(0, 1088, 16)
    kpart = pl.row(0, 1104, 16)
    tmps2 = pl.row(0, 1120, 16)
    ones512 = pl.row(0, 0, 512)
    p.memset(ones512, 1.0)
    p.memset(kssmax, 0.0)
    p.memset(qssmax, 0.0)

    def load_bcols(b_d, dst, n):
        bt = b_d.rearrange("(c r) -> c r", r=128)
        for c in range(n):
            p.dma(dst[:, c:c + 1], bt[c:c + 1, :].rearrange("a r -> r a"))

    def stage1(chain, n):
        w1_d = wd[chain + "1T"]
        for oc in range(NCHUNK):
            p.dma_cast(w1t.rearrange("p (c o) -> p c o", o=128),
                       w1_d.rearrange("(c r) o -> r c o", r=128)[:, :, oc * 128:(oc + 1) * 128])
            for c in range(NCHUNK):
                p.mm(s1ps, w1t[:, c * 128:(c + 1) * 128],
                     pl.xn[:, c * T + n * 512:c * T + (n + 1) * 512],
                     start=(c == 0), stop=(c == NCHUNK - 1))
            p.act(xP[:, oc * 512:(oc + 1) * 512], s1ps, AF.Identity,
                  bias=biases[:, oc:oc + 1])

    def stats_from_evac(dstmax, kc):
        p.act(sqb, ev[:, 0:512], AF.Square)
        p.mm(ssps[0:1, 0:512], cm.onescol_b[0:64, 0:1], sqb[0:64, :], start=True, stop=True)
        p.mm(ssps[0:1, 512:1024], cm.onescol_b[64:128, 0:1], sqb[64:128, :], start=True, stop=True)
        for hh in range(2):
            h = kc * 2 + hh
            p.red(tmp1, ssps[0:1, hh * 512:(hh + 1) * 512], ALU.max)
            p.tt(dstmax[0:1, h:h + 1], dstmax[0:1, h:h + 1], tmp1, ALU.max)

    load_bcols(wd["bv"], biases, 16)
    p.dma_cast(browb, wd["bvl"].unsqueeze(0))
    for c in range(NCHUNK):
        load_w(p, wd["v2T"], w2[:, c * 1024:(c + 1) * 1024], c, slice(0, 1024))
    for n in range(4):
        stage1("v", n)
        for tb in range(4):
            for half in range(2):
                ocs = slice(half * 512, (half + 1) * 512)
                for c in range(NCHUNK):
                    p.mm(s2ps, xP[:, c * 512 + tb * 128: c * 512 + (tb + 1) * 128],
                         w2[:, c * 1024 + half * 512: c * 1024 + half * 512 + 512],
                         start=(c == 0), stop=False)
                p.mm(s2ps, cm.onesb[:], browb[0:1, ocs], start=False, stop=True)
                p.copy(ev[:, 0:512], s2ps, eng="scalar")
                trow = n * 512 + tb * 128
                p.pdma(vsh[trow:trow + 128, ocs], vsh[2048 + trow:2048 + trow + 128, ocs],
                       ev[:, 0:512], ev[:, 0:512])

    load_bcols(wd["bk"], biases, 16)
    load_bcols(wd["bkl"], b2c, 8)
    for c in range(NCHUNK):
        load_w(p, wd["k2T"], w2[:, c * 1024:(c + 1) * 1024], c, slice(0, 1024))
    for n in range(4):
        stage1("k", n)
        for kc in range(8):
            for c in range(NCHUNK):
                p.mm(s2ps, w2[:, c * 1024 + kc * 128: c * 1024 + (kc + 1) * 128],
                     xP[:, c * 512:(c + 1) * 512], start=(c == 0), stop=(c == NCHUNK - 1))
            p.act(ev[:, 0:512], s2ps, AF.Identity, bias=b2c[:, kc:kc + 1])
            krow = kc * 128
            p.pdma(ksh[krow:krow + 128, n * 512:(n + 1) * 512],
                   ksh[1024 + krow:1024 + krow + 128, n * 512:(n + 1) * 512],
                   ev[:, 0:512], ev[:, 0:512])
            stats_from_evac(kssmax, kc)
    p.pdma(stsh[0:1, :], stsh[1:2, :], kssmax, kssmax)
    shr.set_flag(cm, Shr.FL_L0AA)

    load_bcols(wd["bq"], biases, 16)
    load_bcols(wd["bql"], b2c, 8)
    for c in range(NCHUNK):
        load_w(p, wd["q2T"], w2[:, c * 1024:(c + 1) * 1024], c, slice(0, 1024))
    for n in range(4):
        stage1("q", n)
        for kc in range(8):
            for c in range(NCHUNK):
                p.mm(s2ps, w2[:, c * 1024 + kc * 128: c * 1024 + (kc + 1) * 128],
                     xP[:, c * 512:(c + 1) * 512], start=(c == 0), stop=(c == NCHUNK - 1))
            p.act(ev[:, 0:512], s2ps, AF.Identity, bias=b2c[:, kc:kc + 1])
            p.dma(qdr[kc * 128:(kc + 1) * 128, n * 512:(n + 1) * 512], ev[:, 0:512])
            stats_from_evac(qssmax, kc)

    shr.wait_flag(cm, Shr.FL_L0AA)
    p.pdma(kpart, kpart, stsh[1:2, :], stsh[0:1, :])
    p.tt(kssmax, kssmax, kpart, ALU.max)
    p.tt(tmps, kssmax, qssmax, ALU.mult)
    p.act(tmps2, tmps, AF.Ln, bias=cm.sc(SC_L2EPS))
    p.act(tmps, tmps2, AF.Exp, scale=0.5)
    p.ts(negC, tmps, -1.0, ALU.mult)

    emit_attn_core(p, cm, pl, tag, qdr, xattn, ksh, vsh,
                   nheads=16, hd=64, att_scale=0.125, causal=True,
                   mode="bound", negC_bf=negC, amask_sb=amask_sb)

    for c in range(8):
        load_w(p, wd["oT"], w2[:, c * 2048:(c + 1) * 2048], c, slice(0, 2048))
    load_bcols(wd["bo"], biases, 16)
    xres = pl.f[:, 1024:1536]
    osb2 = pl.f[:, 1536:2048]
    for n in range(4):
        cs = slice(n * 512, (n + 1) * 512)
        for oc in range(NCHUNK):
            for c in range(8):
                p.mm(s2ps, w2[:, c * 2048 + oc * 128: c * 2048 + (oc + 1) * 128],
                     xattn[:, c * T + n * 512: c * T + (n + 1) * 512],
                     start=(c == 0), stop=(c == 7))
            p.act(osb2, s2ps, AF.Identity, bias=biases[:, oc:oc + 1])
            p.dma(xres, x_d[oc * 128:(oc + 1) * 128, cs])
            p.tt(osb2, osb2, xres, ALU.add)
            p.dma(xout_d[oc * 128:(oc + 1) * 128, cs], osb2)


def prep_l0aa(inputs, parity):
    w = {}
    w["v1T"] = _f(inputs["L0_attn_a_wv"])
    w["k1T"] = _f(inputs["L0_attn_a_wk"])
    w["q1T"] = _f(inputs["L0_attn_a_wq"])
    w["v2T"] = _f(inputs["L0_attn_a_wvl"])
    w["k2T"] = _f(inputs["L0_attn_a_wkl"])
    w["q2T"] = _f(inputs["L0_attn_a_wql"])
    w["oT"] = _f(inputs["L0_attn_a_wo"])
    w["bv"] = _f(inputs["L0_attn_a_bv"])
    w["bk"] = _f(inputs["L0_attn_a_bk"])
    w["bq"] = _f(inputs["L0_attn_a_bq"])
    w["bvl"] = _f(inputs["L0_attn_a_bvl"])
    w["bkl"] = _f(inputs["L0_attn_a_bkl"])
    w["bql"] = _f(inputs["L0_attn_a_bql"])
    w["bo"] = _f(inputs["L0_attn_a_bo"])
    w["amask"] = build_amask(parity)
    return w


def build_amask(par, dtype=None):
    if dtype is None:
        dtype = NP16
    m = np.zeros((128, 8, 512), np.float32)
    for o in range(4):
        for s in range(2):
            for qq in range(4):
                rel = o - qq
                blk = np.zeros((128, 128), np.float32)
                if s == par:
                    if rel < 0:
                        blk[:] = 1.0
                    elif rel == 0:
                        blk = np.tril(np.ones((128, 128), np.float32)).T
                else:
                    if par == 0:
                        blk[:] = 1.0 if rel < 0 else 0.0
                    else:
                        blk[:] = 1.0 if rel <= 0 else 0.0
                m[:, o * 2 + s, qq * 128:(qq + 1) * 128] = blk
    out = np.ascontiguousarray(m.reshape(128, 8 * 512)).astype(dtype)
    if dtype is not NP16:
        out = out.view(NP16)
    return out

def emit_attn_std(p: Prog, cm: Common, pl: Pools, shr: Shr, x_d, xout_d, wd, cfg):
    tag = cfg["tag"]
    nheads, hd = cfg["nheads"], cfg["hd"]
    AH = nheads * hd
    AHC = AH // 128
    causal, mode = cfg["causal"], cfg["mode"]

    wcur = pl.w[:, 0:32768]
    ev = pl.ab[:, 8192:8704]
    ev2 = pl.ab[:, 8704:9216]
    sqb = pl.ab[:, 9216:9728]
    rop1 = pl.ab[:, 9728:10240]
    biases = pl.f[:, 2100:2116]
    browb = pl.ab[0:1, 4224:4224 + AH]
    s2ps = pl.pp0[:, 512:1024]
    rqp = pl.pp1[:, 1024:1536]
    ssps = pl.pp1[0:1, 0:1024]
    pv_bf = cfg.get("pv_bf", False)
    pvdt = BF16 if pv_bf else LP16
    qdr = p.dram(tag + "_q", [AH, 2048], LP16)
    ksh = p.dram(tag + "_ksh", [2 * AH, 2048], LP16, addr_space="Shared")
    vsh = p.dram(tag + "_vsh", [2 * 2048, AH], pvdt, addr_space="Shared")
    stsh = p.dram(tag + "_st", [2, nheads], F32, addr_space="Shared")
    _DBG.update(qdr=qdr, ksh=ksh, vsh=vsh)

    amask_sb = None
    if causal:
        amask_sb = pl.ab[:, 10752:14848]
        p.dma(amask_sb, wd["amask"])
    if cfg["rope"]:
        costab = pl.ab[:, 0:2048]
        sintab = pl.ab[:, 2048:4096]
        rT = pl.ab[:, 6400:6528]
        p.dma(costab, wd["costab"])
        p.dma(sintab, wd["sintab"])
        p.dma(rT, wd["rT"])

    kssmax = pl.row(0, 1024, nheads)
    qssmax = pl.row(0, 1024 + 32, nheads)
    tmps = pl.row(0, 1024 + 64, nheads)
    tmps2 = pl.row(0, 1024 + 96, nheads)
    tmp1 = pl.row(0, 1024 + 128, 1)
    negC = pl.row(0, 1024 + 160, nheads)
    kpart = pl.row(0, 1024 + 192, nheads)
    ones512 = pl.row(0, 0, 512)
    p.memset(ones512, 1.0)
    p.memset(kssmax, 0.0)
    p.memset(qssmax, 0.0)

    def load_bcols(b_d, dst, n):
        bt = b_d.rearrange("(c r) -> c r", r=128)
        for c in range(n):
            p.dma(dst[:, c:c + 1], bt[c:c + 1, :].rearrange("a r -> r a"))

    def load_wcur(w_d, ncols):
        for c in range(NCHUNK):
            load_w(p, w_d, wcur[:, c * ncols:(c + 1) * ncols], c, slice(0, ncols))

    def rope_apply(n):
        cs = slice(n * 512, (n + 1) * 512)
        p.mm(rqp[:, 0:512], rT, ev, start=True, stop=True)
        p.tt(rop1, ev, costab[:, cs], ALU.mult)
        p.tt(ev2, rqp[:, 0:512], sintab[:, cs], ALU.mult)
        p.tt(ev2, rop1, ev2, ALU.add)

    def stats_from_evac(dstmax, kc):
        p.act(sqb, ev2 if cfg["rope"] else ev, AF.Square)
        if hd == 64:
            p.mm(ssps[0:1, 0:512], cm.onescol_b[0:64, 0:1], sqb[0:64, :], start=True, stop=True)
            p.mm(ssps[0:1, 512:1024], cm.onescol_b[64:128, 0:1], sqb[64:128, :], start=True, stop=True)
            pairs = [(kc * 2, 0), (kc * 2 + 1, 512)]
        else:
            p.mm(ssps[0:1, 0:512], cm.onescol_b[:], sqb, start=True, stop=True)
            pairs = [(kc, 0)]
        for h, off in pairs:
            p.red(tmp1, ssps[0:1, off:off + 512], ALU.max)
            p.tt(dstmax[0:1, h:h + 1], dstmax[0:1, h:h + 1], tmp1, ALU.max)

    evv = ev.bitcast(BF16) if pv_bf else ev
    load_wcur(wd["vT"], 2048)
    p.dma_cast(browb, wd["bv"].unsqueeze(0))
    for n in range(4):
        for tb in range(4):
            for half in range(AH // 512):
                ocs = slice(half * 512, (half + 1) * 512)
                for c in range(NCHUNK):
                    p.mm(s2ps, pl.xn[:, c * T + n * 512 + tb * 128: c * T + n * 512 + (tb + 1) * 128],
                         wcur[:, c * 2048 + half * 512: c * 2048 + half * 512 + 512],
                         start=(c == 0), stop=False)
                p.mm(s2ps, cm.onesb[:], browb[0:1, ocs], start=False, stop=True)
                p.copy(evv[:, 0:512], s2ps, eng="scalar")
                trow = n * 512 + tb * 128
                p.pdma(vsh[trow:trow + 128, ocs], vsh[2048 + trow:2048 + trow + 128, ocs],
                       evv[:, 0:512], evv[:, 0:512])

    load_wcur(wd["kT"], 2048)
    load_bcols(wd["bk"], biases, 16)
    for n in range(4):
        for kc in range(AHC):
            for c in range(NCHUNK):
                p.mm(s2ps, wcur[:, c * 2048 + kc * 128: c * 2048 + (kc + 1) * 128],
                     pl.xn[:, c * T + n * 512:c * T + (n + 1) * 512],
                     start=(c == 0), stop=(c == NCHUNK - 1))
            p.act(ev, s2ps, AF.Identity, bias=biases[:, kc:kc + 1])
            if cfg["rope"]:
                rope_apply(n)
            src = ev2 if cfg["rope"] else ev
            krow = kc * 128
            p.pdma(ksh[krow:krow + 128, n * 512:(n + 1) * 512],
                   ksh[AH + krow:AH + krow + 128, n * 512:(n + 1) * 512], src, src)
            if mode == "bound":
                stats_from_evac(kssmax, kc)
    if mode == "bound":
        p.pdma(stsh[0:1, :], stsh[1:2, :], kssmax, kssmax)
    shr.set_flag(cm, cfg["flag"])

    load_wcur(wd["qT"], 2048)
    load_bcols(wd["bq"], biases, 16)
    for n in range(4):
        for kc in range(AHC):
            for c in range(NCHUNK):
                p.mm(s2ps, wcur[:, c * 2048 + kc * 128: c * 2048 + (kc + 1) * 128],
                     pl.xn[:, c * T + n * 512:c * T + (n + 1) * 512],
                     start=(c == 0), stop=(c == NCHUNK - 1))
            p.act(ev, s2ps, AF.Identity, bias=biases[:, kc:kc + 1])
            if cfg["rope"]:
                rope_apply(n)
            src = ev2 if cfg["rope"] else ev
            p.dma(qdr[kc * 128:(kc + 1) * 128, n * 512:(n + 1) * 512], src)
            if mode == "bound":
                stats_from_evac(qssmax, kc)

    shr.wait_flag(cm, cfg["flag"])
    negC_ap = None
    if mode == "bound":
        p.pdma(kpart, kpart, stsh[1:2, :], stsh[0:1, :])
        p.tt(kssmax, kssmax, kpart, ALU.max)
        p.tt(tmps, kssmax, qssmax, ALU.mult)
        p.act(tmps2, tmps, AF.Ln, bias=cm.sc(SC_L2EPS))
        p.act(tmps, tmps2, AF.Exp, scale=0.5)
        p.ts(negC, tmps, -1.0, ALU.mult)
        negC_ap = negC

    xattn = pl.xn[:, 0:AHC * T]
    emit_attn_core(p, cm, pl, tag, qdr, xattn, ksh, vsh,
                   nheads=nheads, hd=hd, att_scale=cfg["scale"], causal=causal,
                   mode=mode, negC_bf=negC_ap, amask_sb=amask_sb, pv_bf=pv_bf)

    load_wcur(wd["oT"], 2048)
    load_bcols(wd["bo"], biases, 16)
    xres = pl.f[:, 1024:1536]
    osb2 = pl.f[:, 1536:2048]
    for n in range(4):
        cs = slice(n * 512, (n + 1) * 512)
        for oc in range(NCHUNK):
            for c in range(AHC):
                p.mm(s2ps, wcur[:, c * 2048 + oc * 128: c * 2048 + (oc + 1) * 128],
                     xattn[:, c * T + n * 512: c * T + (n + 1) * 512],
                     start=(c == 0), stop=(c == AHC - 1))
            p.act(osb2, s2ps, AF.Identity, bias=biases[:, oc:oc + 1])
            p.dma(xres, x_d[oc * 128:(oc + 1) * 128, cs])
            p.tt(osb2, osb2, xres, ALU.add)
            p.dma(xout_d[oc * 128:(oc + 1) * 128, cs], osb2)


def rope_tables(parity):
    pos = ((2 * np.arange(16)[:, None] + parity) * 128 + np.arange(128)).reshape(T)
    inv = 1.0 / (10000.0 ** (np.arange(0, 32, 2) / 32.0))
    fr = pos[:, None] * inv[None, :]
    emb = np.concatenate([fr, fr], axis=1)
    cos = np.ones((128, T), np.float32)
    sin = np.zeros((128, T), np.float32)
    cos[0:32, :] = np.cos(emb).T
    sin[0:32, :] = np.sin(emb).T
    R = np.zeros((128, 128), np.float32)
    for m in range(16):
        R[m, 16 + m] = -1.0
        R[16 + m, m] = 1.0
    return cos.astype(NP16), sin.astype(NP16), np.ascontiguousarray(R.T).astype(NP16)


def prep_l0ab(inputs, parity):
    w = {}
    wqkv = _f(inputs["L0_attn_b_w_qkv"]).reshape(D, 16, 3, 128)
    bqkv = _f(inputs["L0_attn_b_b_qkv"]).reshape(16, 3, 128)
    w["qT"] = _f(wqkv[:, :, 0, :].reshape(D, D))
    w["kT"] = _f(wqkv[:, :, 1, :].reshape(D, D))
    w["vT"] = _f(wqkv[:, :, 2, :].reshape(D, D))
    w["bq"] = _f(bqkv[:, 0, :].reshape(D))
    w["bk"] = _f(bqkv[:, 1, :].reshape(D))
    w["bv"] = _f(bqkv[:, 2, :].reshape(D))
    w["oT"] = _f(inputs["L0_attn_b_w_out"])
    w["bo"] = _f(inputs["L0_attn_b_b_out"])
    w["amask"] = build_amask(parity, dtype=BF)
    cos, sin, rT = rope_tables(parity)
    w["costab"], w["sintab"], w["rT"] = cos, sin, rT
    return w


CFG_L0AB = dict(tag="l0ab", nheads=16, hd=128, scale=1.0, causal=True,
                mode="max", flag=Shr.FL_L0AB, rope=True, pv_bf=True)
CFG_L1AA = dict(tag="l1aa", nheads=32, hd=64, scale=1.0 / np.sqrt(32.0),
                causal=False, mode="bound", flag=Shr.FL_L1AA, rope=False,
                pv_bf=True)


def prep_l1aa(inputs, parity):
    w = {}
    w["qT"] = _f(inputs["L1_attn_a_Wq"])
    w["kT"] = _f(inputs["L1_attn_a_Wk"])
    w["vT"] = _f(inputs["L1_attn_a_Wv"])
    w["oT"] = _f(inputs["L1_attn_a_Wo"])
    w["bq"] = _f(inputs["L1_attn_a_bq"])
    w["bk"] = _f(inputs["L1_attn_a_bk"])
    w["bv"] = _f(inputs["L1_attn_a_bv"])
    w["bo"] = _f(inputs["L1_attn_a_bo"])
    return w



def _declare(p, prefix, arrays):
    out = {}
    for k, v in arrays.items():
        if isinstance(v, list):
            out[k] = [p.inp(f"{prefix}{k}{i}", list(a.shape),
                            LP16 if a.dtype == np.float16 else F32)
                      for i, a in enumerate(v)]
        else:
            out[k] = p.inp(f"{prefix}{k}", list(v.shape),
                           LP16 if v.dtype == np.float16 else F32)
    return out


def _feed(im, prefix, arrays):
    for k, v in arrays.items():
        if isinstance(v, list):
            for i, a in enumerate(v):
                im[f"{prefix}{k}{i}"] = a
        else:
            im[f"{prefix}{k}"] = v


def build_launchA(shapes_only_preps):
    p = Prog()
    cm = Common(p)
    x0 = p.inp("x", [D, T], F32)
    x5o = p.out("x5o", [D, T], F32)
    dmem_o = p.out("dmem", [NU_F, 2048], F32)
    dnorm_o = p.out("dnorm", [NU_F, 16], F32)
    wds = {s: _declare(p, s + "_", arrs) for s, arrs in shapes_only_preps.items()}
    pl = Pools(p)
    shr = Shr(p)
    mk = p.out if DEBUG_STREAMS else (lambda n, s, d: p.dram(n, s, d))
    x1 = mk("x1", [D, T], F32)
    x2 = mk("x2", [D, T], F32)
    x3 = mk("x3", [D, T], F32)
    x4 = mk("x4", [D, T], F32)
    x5i = mk("x5i", [D, T], F32)
    xn_dram = p.dram("xnd", [D, T], LP16)

    shr.hello(cm)
    emit_layernorm(p, cm, pl, x0)
    emit_l0_attn_a(p, cm, pl, shr, x0, x1, wds["l0aa"])
    emit_layernorm(p, cm, pl, x1)
    emit_l0ffn(p, cm, pl, x1, x2, wds["l0ffn"])
    emit_layernorm(p, cm, pl, x2)
    emit_attn_std(p, cm, pl, shr, x2, x3, wds["l0ab"], CFG_L0AB)
    emit_layernorm(p, cm, pl, x3)
    emit_l0ffn(p, cm, pl, x3, x4, wds["l0ffn"])
    emit_layernorm(p, cm, pl, x4)
    emit_attn_std(p, cm, pl, shr, x4, x5i, wds["l1aa"], CFG_L1AA)
    emit_layernorm(p, cm, pl, x5i, xn_dram=xn_dram)
    emit_l1ffn(p, cm, pl, x5i, xn_dram, x5o, wds["l1ffn"], "l1f")
    emit_layernorm(p, cm, pl, x5o)
    emit_l1ab_p1(p, cm, pl, x5o, wds["l1ab"], dmem_o, dnorm_o)
    return p.finalize(), len(p.ops)


def build_launchB(shapes_only_preps):
    p = Prog()
    cm = Common(p)
    x5 = p.inp("x5", [D, T], F32)
    y = p.out("y", [D, T], F32)
    parts = {"dmem": [p.inp(f"dmem{r}", [NU_F, 2048], F32) for r in range(8)],
             "dnorm": [p.inp(f"dnorm{r}", [NU_F, 16], F32) for r in range(8)]}
    wds = {s: _declare(p, s + "_", arrs) for s, arrs in shapes_only_preps.items()}
    pl = Pools(p)
    x6 = p.dram("x6", [D, T], F32)
    xn_dram = p.dram("xnd", [D, T], LP16)

    emit_layernorm(p, cm, pl, x5)
    emit_l1ab_p2(p, cm, pl, x5, x6, wds["l1ab"], parts)
    emit_layernorm(p, cm, pl, x6, xn_dram=xn_dram)
    emit_l1ffn(p, cm, pl, x6, xn_dram, y, wds["l1ffn"], "l1f")
    return p.finalize(), len(p.ops)


def kernel(**inputs):
    import os as _os
    consts, crow = host_consts()
    identb, identf = host_ident()
    nonce = np.array([[int.from_bytes(_os.urandom(4), "little") & 0x3FFFFFFF | 1,
                       0, 0, 0]], np.int32)

    preps_par = [
        {"l0aa": prep_l0aa(inputs, par), "l0ffn": prep_l0ffn(inputs),
         "l0ab": prep_l0ab(inputs, par), "l1aa": prep_l1aa(inputs, par),
         "l1ffn": prep_l1ffn(inputs), "l1ab": prep_l1ab(inputs)}
        for par in (0, 1)
    ]

    ncA, nopsA = build_launchA(preps_par[0])
    shards = []
    for b in range(4):
        ev, od = shard_tokens(np.asarray(inputs["L0_attn_a_x"][b]))
        shards.extend([ev, od])
    in_mapsA = []
    for core in range(8):
        par = core & 1
        im = dict(x=shards[core], consts=consts, crow=crow, ident_bf=identb,
                  ident_f32=identf, nonce=nonce)
        for s, arrs in preps_par[par].items():
            _feed(im, s + "_", arrs)
        in_mapsA.append(im)
    resA = run_bass_kernel_spmd(ncA, in_mapsA, core_ids=list(range(8)))

    ncB, nopsB = build_launchB({k: preps_par[0][k] for k in ("l1ffn", "l1ab")})
    in_mapsB = []
    for core in range(8):
        par = core & 1
        im = dict(x5=np.asarray(resA.results[core]["x5o"]), consts=consts,
                  crow=crow, ident_bf=identb, ident_f32=identf, nonce=nonce)
        for r in range(8):
            im[f"dmem{r}"] = np.asarray(resA.results[r]["dmem"])
            im[f"dnorm{r}"] = np.asarray(resA.results[r]["dnorm"])
        for s in ("l1ffn", "l1ab"):
            _feed(im, s + "_", preps_par[par][s])
        in_mapsB.append(im)
    resB = run_bass_kernel_spmd(ncB, in_mapsB, core_ids=list(range(8)))

    out = np.empty((4, 4096, D), np.float32)
    for b in range(4):
        out[b] = unshard_tokens(resB.results[2 * b]["y"], resB.results[2 * b + 1]["y"])
    return out


def emit_l1ffn(p: Prog, cm: Common, pl: Pools, x_d, xn_d, xout_d, wd, tag):
    wcur = pl.w[:, 0:32768]
    xnsl = pl.xn[:, 16384:24576]
    xnv = xnsl.rearrange("p (c t) -> p c t", t=512)
    racc = pl.xn[:, 0:16384].bitcast(F32)
    cw_f = pl.rows[0:9, 0:2048]
    esel = pl.rows[0:9, 2048:3200]
    hps = pl.pp0[:, 0:512]
    dps = pl.pp0[:, 512:1024]
    cwx = pl.pp1[:, 0:512]
    scps = pl.pp1[:, 1024:1536]
    trps = pl.pp1[0:9, 512:1024]
    gse = pl.ab[:, 15360:15504]
    m1 = pl.f[:, 2048:2049]
    nm1 = pl.f[:, 2049:2050]
    m2 = pl.f[:, 2050:2051]
    ll = pl.f[:, 2051:2052]
    rl = pl.f[:, 2052:2053]
    ex = pl.f[:, 2056:2064]
    msk = pl.f[:, 2064:2072]
    s2 = pl.f[:, 2072:2080]
    ww = pl.f[:, 2080:2088]
    comb9 = pl.f[:, 2088:2097]
    xres = pl.f[:, 1024:1536]
    osb = pl.f[:, 1536:2048]

    p.dma(esel, wd["esel"])
    for c in range(NCHUNK):
        load_w(p, wd["gwseT"], gse[:, c * 9:(c + 1) * 9], c, slice(0, 9))

    for n in range(4):
        p.dma(xnsl, xn_d.rearrange("(c p) t -> p c t", p=128)[:, :, n * 512:(n + 1) * 512])
        for tc4 in range(4):
            tc = n * 4 + tc4
            for c in range(NCHUNK):
                p.mm(scps[:, 0:9], xnv[:, c, tc4 * 128:(tc4 + 1) * 128],
                     gse[:, c * 9:(c + 1) * 9], start=(c == 0), stop=(c == NCHUNK - 1))
            sc8 = scps[:, 0:8]
            p.red(m1, sc8, ALU.max)
            p.ts(nm1, m1, -1.0, ALU.mult)
            p.act(ex, sc8, AF.Exp, bias=nm1)
            p.ts(msk, sc8, m1, ALU.is_ge)
            p.ts(s2, msk, -1e30, ALU.mult)
            p.tt(s2, sc8, s2, ALU.add)
            p.red(m2, s2, ALU.max)
            p.ts(msk, sc8, m2, ALU.is_ge)
            p.tt(ww, ex, msk, ALU.mult)
            p.red(ll, ww, ALU.add)
            p.recip(rl, ll)
            p.ts(comb9[:, 0:8], ww, rl, ALU.mult)
            p.act(comb9[:, 8:9], scps[:, 8:9], AF.Sigmoid)
            p.mm(trps[0:9, 0:128], comb9, cm.identf[:], start=True, stop=True)
            p.copy(cw_f[:, tc * 128:(tc + 1) * 128], trps[0:9, 0:128])

    for n in range(4):
        p.dma(xnsl, xn_d.rearrange("(c p) t -> p c t", p=128)[:, :, n * 512:(n + 1) * 512])
        for half, wkey, dst in ((0, "shgT", pl.ab[:, 0:8192]), (1, "shuT", pl.ab[:, 8192:16384])):
            for c in range(NCHUNK):
                load_w(p, wd[wkey], wcur[:, c * 2048:(c + 1) * 2048], c, slice(0, 2048))
            for hc in range(16):
                for c in range(NCHUNK):
                    p.mm(hps, wcur[:, c * 2048 + hc * 128: c * 2048 + (hc + 1) * 128],
                         xnv[:, c, :], start=(c == 0), stop=(c == NCHUNK - 1))
                p.act(dst[:, hc * 512:(hc + 1) * 512], hps,
                      AF.Silu if half == 0 else AF.Identity)
        h1, h3 = pl.ab[:, 0:8192], pl.ab[:, 8192:16384]
        if f"dbg_pre{n}" in wd:
            p.dma(wd[f"dbg_pre{n}"], h1)
            p.dma(wd[f"dbg_xn{n}"], h3)
        p.mm(cwx, esel[0:9, 8 * 128:9 * 128], cw_f[0:9, n * 512:(n + 1) * 512],
             start=True, stop=True)
        for hc in range(16):
            hsl = slice(hc * 512, (hc + 1) * 512)
            p.tt(h1[:, hsl], h1[:, hsl], h3[:, hsl], ALU.mult)
            p.tt(h1[:, hsl], h1[:, hsl], cwx, ALU.mult)
        if f"dbg_hh{n}" in wd:
            p.dma(wd[f"dbg_hh{n}"], h1)
            p.copy(pl.f[:, 1536:2048], cwx)
            p.dma(wd[f"dbg_cwx{n}"], pl.f[:, 1536:2048])
        for c in range(NCHUNK):
            load_w(p, wd["shdT"], wcur[:, c * 2048:(c + 1) * 2048], c, slice(0, 2048))
        for oc in range(16):
            for c in range(NCHUNK):
                p.mm(dps, wcur[:, c * 2048 + oc * 128: c * 2048 + (oc + 1) * 128],
                     h1[:, c * 512:(c + 1) * 512], start=(c == 0), stop=(c == NCHUNK - 1))
            p.copy(racc[:, oc * 512:(oc + 1) * 512], dps)
        if f"dbg_sh{n}" in wd:
            p.dma(wd[f"dbg_sh{n}"], racc)
            p.dma(wd[f"dbg_h1{n}"], h1)
        h1e, h3e = pl.ab[:, 0:4096], pl.ab[:, 4096:8192]
        for e in range(8):
            for half, wkey, dst, fn in ((0, "wgT", h1e, AF.Silu), (1, "wuT", h3e, AF.Identity)):
                wde = wd[wkey][e]
                for c in range(NCHUNK):
                    load_w(p, wde, wcur[:, c * 1024:(c + 1) * 1024], c, slice(0, 1024))
                for hc in range(8):
                    for c in range(NCHUNK):
                        p.mm(hps, wcur[:, c * 1024 + hc * 128: c * 1024 + (hc + 1) * 128],
                             xnv[:, c, :], start=(c == 0), stop=(c == NCHUNK - 1))
                    p.act(dst[:, hc * 512:(hc + 1) * 512], hps, fn)
            p.mm(cwx, esel[0:9, e * 128:(e + 1) * 128], cw_f[0:9, n * 512:(n + 1) * 512],
                 start=True, stop=True)
            for hc in range(8):
                hsl = slice(hc * 512, (hc + 1) * 512)
                p.tt(h1e[:, hsl], h1e[:, hsl], h3e[:, hsl], ALU.mult)
                p.tt(h1e[:, hsl], h1e[:, hsl], cwx, ALU.mult)
            wde = wd["wdT"][e]
            for c in range(8):
                load_w(p, wde, wcur[:, 16384 + c * 2048:16384 + (c + 1) * 2048], c, slice(0, 2048))
            for oc in range(16):
                for c in range(8):
                    p.mm(dps, wcur[:, 16384 + c * 2048 + oc * 128: 16384 + c * 2048 + (oc + 1) * 128],
                         h1e[:, c * 512:(c + 1) * 512], start=(c == 0), stop=(c == 7))
                p.tt(racc[:, oc * 512:(oc + 1) * 512], racc[:, oc * 512:(oc + 1) * 512],
                     dps, ALU.add)
        cs = slice(n * 512, (n + 1) * 512)
        for oc in range(16):
            p.dma(xres, x_d[oc * 128:(oc + 1) * 128, cs])
            p.tt(osb, racc[:, oc * 512:(oc + 1) * 512], xres, ALU.add)
            p.dma(xout_d[oc * 128:(oc + 1) * 128, cs], osb)


def prep_l1ffn(inputs):
    w = {}
    gw = _f(inputs["L1_ffn_gate_w"])
    se = _f(inputs["L1_ffn_se_gate_w"])
    w["gwseT"] = _f(np.concatenate([gw, se], axis=0).T)
    w["wgT"] = [_f(inputs["L1_ffn_w_gate"][e].T) for e in range(8)]
    w["wuT"] = [_f(inputs["L1_ffn_w_up"][e].T) for e in range(8)]
    w["wdT"] = [_f(inputs["L1_ffn_w_down"][e].T) for e in range(8)]
    w["shgT"] = _f(np.asarray(inputs["L1_ffn_sh_gate"]).T)
    w["shuT"] = _f(np.asarray(inputs["L1_ffn_sh_up"]).T)
    w["shdT"] = _f(np.asarray(inputs["L1_ffn_sh_down"]).T)
    esel = np.zeros((9, 9 * 128), np.float32)
    for e in range(9):
        esel[e, e * 128:(e + 1) * 128] = 1.0
    w["esel"] = esel
    return w

NU_F = 768


def emit_l1ab_p1(p: Prog, cm: Common, pl: Pools, x_d, wd, dmem_o, dnorm_o):
    tag = "l1abp1"
    ksb = pl.ab[:, 0:8192]
    vsb = pl.ab[:, 8192:16384]
    wmv = pl.xn
    mbdr = p.dram(tag + "_mb", [T, 2048], LP16)
    mkdr = p.dram(tag + "_mk", [T, 4 * NU_F], LP16)
    s2ps = pl.pp0[:, 512:1024]
    ndps = pl.pp0[:, 0:512]
    tpsf = pl.pp1[:, 512:1024].bitcast(LP16)
    dpsA = pl.pp1[:, 0:512]
    dpsB = pl.pp0[:, 1024:1536]
    brow = pl.w[0:1, 32768:34816]
    biases = pl.f[:, 2100:2116]

    xq = pl.w[:, 0:256]
    mkraw = pl.w[:, 256:1024]
    mkfm = pl.w[:, 1024:1792]
    pvsb = pl.w[:, 1792:1920]
    mbsb = pl.w[:, 1920:2048]
    mvsb = pl.w[:, 2048:2176]
    memnorm = pl.w[:, 2176:14560]
    sstmp = pl.f[:, 512:513]
    rn = pl.f[:, 513:514]
    msq = pl.f[:, 514:515]
    rmsq = pl.f[:, 515:516]
    dcol = pl.f[:, 516:517]
    rd = pl.f[:, 517:518]
    t2c = pl.f[:, 518:519]
    coefs = pl.w[:, 14560:14816]
    def load_bcols(b_d, dst, n):
        bt = b_d.rearrange("(c r) -> c r", r=128)
        for c in range(n):
            p.dma(dst[:, c:c + 1], bt[c:c + 1, :].rearrange("a r -> r a"))

    for c in range(NCHUNK):
        load_w(p, wd["wkT"], pl.w[:, c * 512:(c + 1) * 512], c, slice(0, 512))
        load_w(p, wd["wvT"], pl.w[:, 8192 + c * 512:8192 + (c + 1) * 512], c, slice(0, 512))
    p.dma_cast(brow[0:1, 0:512], wd["bk"].unsqueeze(0))
    p.dma_cast(brow[0:1, 512:1024], wd["bv"].unsqueeze(0))
    for tc in range(NTCH):
        for (base, dst, boff) in ((0, ksb, 0), (8192, vsb, 512)):
            for c in range(NCHUNK):
                p.mm(s2ps, pl.xn[:, c * T + tc * 128:c * T + (tc + 1) * 128],
                     pl.w[:, base + c * 512:base + (c + 1) * 512],
                     start=(c == 0), stop=False)
            p.mm(s2ps, cm.onesb[:], brow[0:1, boff:boff + 512], start=False, stop=True)
            p.copy(dst[:, tc * 512:(tc + 1) * 512], s2ps, eng="scalar")
    for c in range(NCHUNK):
        load_w(p, wd["wmbT"], pl.w[:, c * 2048:(c + 1) * 2048], c, slice(0, 2048))
    p.dma_cast(brow[0:1, 0:2048], wd["bmb"].unsqueeze(0))
    evmb = pl.f[:, 1536:2048].bitcast(LP16)
    for tc in range(NTCH):
        for q4 in range(4):
            ocs = slice(q4 * 512, (q4 + 1) * 512)
            for c in range(NCHUNK):
                p.mm(s2ps, pl.xn[:, c * T + tc * 128:c * T + (tc + 1) * 128],
                     pl.w[:, c * 2048 + q4 * 512:c * 2048 + (q4 + 1) * 512],
                     start=(c == 0), stop=False)
            p.mm(s2ps, cm.onesb[:], brow[0:1, ocs], start=False, stop=True)
            p.act(evmb[:, 0:512], s2ps, AF.Sigmoid)
            p.dma(mbdr[tc * 128:(tc + 1) * 128, ocs], evmb[:, 0:512])

    p.dma_cast(memnorm.rearrange("p (a c) -> p a c", c=2064),
               wd["memnorm"].rearrange("(a p) c -> p a c", p=128))

    for tc in range(NTCH):
        for c in range(4):
            ksl = ksb[:, tc * 512 + c * 128: tc * 512 + (c + 1) * 128]
            p.act(xq[:, 0:128], ksl, AF.Relu)
            p.act(xq[:, 128:256], ksl, AF.Relu, scale=-1.0)
            for r in range(3):
                w1 = 255 - r
                p.tt(mkraw[:, 256 * r + r + 1: 256 * (r + 1)], xq[:, r + 1:256],
                     xq[:, 0:w1], ALU.mult)
                p.tt(mkraw[:, 256 * r: 256 * r + r + 1], xq[:, 0:r + 1],
                     xq[:, 255 - r:256], ALU.mult)
            sqm = pl.f[:, 520:904].bitcast(LP16)
            p.tt(sqm, mkraw, mkraw, ALU.mult)
            p.red(sstmp, sqm, ALU.add)
            p.copy(msq, sstmp)
            p.act(t2c, sstmp, AF.Ln, bias=1e-24)
            p.act(rn, t2c, AF.Exp, scale=-0.5)
            p.ts(mkraw, mkraw, rn, ALU.mult)
            p.ts(t2c, rn, rn, ALU.mult)
            p.tt(msq, msq, t2c, ALU.mult)
            p.recip(rmsq, msq)
            p.dma(mkdr[tc * 128:(tc + 1) * 128, c * NU_F:(c + 1) * NU_F], mkraw)
            for fc in range(6):
                p.transp(tpsf[:, 0:128], mkraw[:, fc * 128:(fc + 1) * 128], cm.ident[:])
                p.copy(mkfm[:, fc * 128:(fc + 1) * 128], tpsf[:, 0:128], eng="scalar")
            for h in range(4 * c, 4 * c + 4):
                for fc in range(6):
                    p.mm(ndps[:, 0:129], mkfm[:, fc * 128:(fc + 1) * 128],
                         memnorm[:, (fc * 16 + h) * 129:(fc * 16 + h + 1) * 129],
                         start=(fc == 0), stop=(fc == 5))
                p.ts(dcol, ndps[:, 128:129], 1e-8, ALU.add)
                p.recip(rd, dcol)
                p.ts(pvsb, ndps[:, 0:128], rd, ALU.mult)
                p.dma(mbsb, mbdr[tc * 128:(tc + 1) * 128, h * 128:(h + 1) * 128])
                vsl = vsb[:, tc * 512 + c * 128: tc * 512 + (c + 1) * 128]
                p.tt(mvsb, vsl, pvsb, ALU.subtract)
                p.tt(wmv[:, tc * 2048 + h * 128: tc * 2048 + (h + 1) * 128],
                     mvsb, mbsb, ALU.mult)
                p.ts(t2c, dcol, rmsq, ALU.mult)
                p.ts(t2c, t2c, -1.0, ALU.mult, 1.0, ALU.add)
                p.ts(coefs[:, tc * 16 + h: tc * 16 + h + 1], t2c, 0.0, ALU.max, 1.0, ALU.min)

    mkt = pl.w[:, 0:128]
    wvt = pl.xn
    evd = pl.f[:, 1536:2048]
    for c in range(4):
        for fc in range(6):
            for tc in range(NTCH):
                p.dma(mkt, mkdr[tc * 128:(tc + 1) * 128, c * NU_F + fc * 128: c * NU_F + (fc + 1) * 128])
                p.mm(dpsA, mkt, wmv[:, tc * 2048 + 4 * c * 128: tc * 2048 + (4 * c + 4) * 128],
                     start=(tc == 0), stop=(tc == NTCH - 1))
                p.mm(dpsB[:, 0:4], mkt,
                     coefs[:, tc * 16 + 4 * c: tc * 16 + 4 * c + 4],
                     start=(tc == 0), stop=(tc == NTCH - 1))
            p.copy(evd, dpsA)
            p.dma(dmem_o[fc * 128:(fc + 1) * 128, 4 * c * 128:(4 * c + 4) * 128], evd)
            p.copy(evd[:, 0:4], dpsB[:, 0:4])
            p.dma(dnorm_o[fc * 128:(fc + 1) * 128, 4 * c:4 * c + 4], evd[:, 0:4])


def prep_l1ab(inputs):
    w = {}
    w["wkT"] = _f(np.asarray(inputs["L1_attn_b_Wk"]).T)
    w["wvT"] = _f(np.asarray(inputs["L1_attn_b_Wv"]).T)
    w["wmbT"] = _f(np.asarray(inputs["L1_attn_b_Wmb"]).T)
    w["wqT"] = _f(np.asarray(inputs["L1_attn_b_Wq"]).T)
    w["wgT"] = _f(np.asarray(inputs["L1_attn_b_Wg"]).T)
    w["bk"] = _f(inputs["L1_attn_b_bk"])
    w["bv"] = _f(inputs["L1_attn_b_bv"])
    w["bmb"] = _f(inputs["L1_attn_b_bmb"])
    w["bq"] = _f(inputs["L1_attn_b_bq"])
    w["bg"] = _f(inputs["L1_attn_b_bg"])
    mem = _f(inputs["L1_attn_b_memory"])
    nrm = _f(inputs["L1_attn_b_norm"])
    mn = np.zeros((NU_F, 16 * 129), np.float32)
    for h in range(16):
        mn[:, h * 129:h * 129 + 128] = mem[h]
        mn[:, h * 129 + 128] = nrm[h]
    w["memnorm"] = _f(mn)
    return w

def emit_l1ab_p2(p: Prog, cm: Common, pl: Pools, x_d, xout_d, wd, parts):
    tag = "l1abp2"
    qdr = p.dram(tag + "_q", [T, 2048], LP16)
    s2ps = pl.pp0[:, 512:1024]
    ndps = pl.pp0[:, 0:512]
    tpsf = pl.pp1[:, 512:1024].bitcast(LP16)
    brow2 = pl.ab[0:1, 0:2048]
    biases = pl.f[:, 2100:2116]

    def load_bcols(b_d, dst, n):
        bt = b_d.rearrange("(c r) -> c r", r=128)
        for c in range(n):
            p.dma(dst[:, c:c + 1], bt[c:c + 1, :].rearrange("a r -> r a"))

    for c in range(NCHUNK):
        load_w(p, wd["wqT"], pl.w[:, c * 2048:(c + 1) * 2048], c, slice(0, 2048))
    p.dma_cast(brow2, wd["bq"].unsqueeze(0))
    for tc in range(NTCH):
        for q4 in range(4):
            ocs = slice(q4 * 512, (q4 + 1) * 512)
            for c in range(NCHUNK):
                p.mm(s2ps, pl.xn[:, c * T + tc * 128:c * T + (tc + 1) * 128],
                     pl.w[:, c * 2048 + q4 * 512:c * 2048 + (q4 + 1) * 512],
                     start=(c == 0), stop=False)
            p.mm(s2ps, cm.onesb[:], brow2[0:1, ocs], start=False, stop=True)
            p.copy(pl.f[:, 0:512].bitcast(LP16)[:, 0:512], s2ps, eng="scalar")
            p.dma(qdr[tc * 128:(tc + 1) * 128, ocs], pl.f[:, 0:512].bitcast(LP16)[:, 0:512])

    nm = pl.w[:, 0:12384]
    accm = pl.rows[:, 0:2048]
    prtb = pl.rows[:, 2048:4096]
    accn = pl.f[:, 0:16]
    prtn = pl.f[:, 16:32]
    mview = wd["memnorm"].rearrange("f (h c) -> f h c", c=129)
    for fc in range(6):
        rows = slice(fc * 128, (fc + 1) * 128)
        p.dma(accm.rearrange("p (h c) -> p h c", c=128), mview[rows, :, 0:128])
        p.dma(accn.rearrange("p (h c) -> p h c", c=1), mview[rows, :, 128:129],
              allow_slow=True)
        for r in range(len(parts["dmem"])):
            p.dma(prtb, parts["dmem"][r][rows, :])
            p.tt(accm, accm, prtb, ALU.add)
            p.dma(prtn, parts["dnorm"][r][rows, :])
            p.tt(accn, accn, prtn, ALU.add)
        for h in range(16):
            p.copy(nm[:, (fc * 16 + h) * 129:(fc * 16 + h) * 129 + 128],
                   accm[:, h * 128:(h + 1) * 128])
            p.copy(nm[:, (fc * 16 + h) * 129 + 128:(fc * 16 + h) * 129 + 129],
                   accn[:, h:h + 1])

    xq = pl.w[:, 12384:12640]
    mkraw = pl.w[:, 12640:13408]
    mkfm = pl.w[:, 13408:14176]
    w1t = pl.w[:, 14176:16224]
    qsl = pl.w[:, 16224:16352]
    mhs = pl.w[:, 16352:16480]
    mh_fm = pl.ab[:, 0:8192]
    gsb = pl.ab[:, 8192:8704]
    sqm = pl.f[:, 520:904].bitcast(LP16)
    sstmp = pl.f[:, 512:513]
    rn = pl.f[:, 513:514]
    t2c = pl.f[:, 514:515]
    dcol = pl.f[:, 516:517]
    rd = pl.f[:, 517:518]
    t1 = pl.f[:, 1024:1536]
    xres = pl.f[:, 1536:2048]
    load_bcols(wd["bg"], biases, 16)

    for n in range(4):
        for tcw in range(4):
            tc = 4 * n + tcw
            for h in range(16):
                p.dma(qsl, qdr[tc * 128:(tc + 1) * 128, h * 128:(h + 1) * 128])
                p.act(xq[:, 0:128], qsl, AF.Relu)
                p.act(xq[:, 128:256], qsl, AF.Relu, scale=-1.0)
                for r in range(3):
                    w1 = 255 - r
                    p.tt(mkraw[:, 256 * r + r + 1: 256 * (r + 1)], xq[:, r + 1:256],
                         xq[:, 0:w1], ALU.mult)
                    p.tt(mkraw[:, 256 * r: 256 * r + r + 1], xq[:, 0:r + 1],
                         xq[:, 255 - r:256], ALU.mult)
                p.tt(sqm, mkraw, mkraw, ALU.mult)
                p.red(sstmp, sqm, ALU.add)
                p.act(t2c, sstmp, AF.Ln, bias=1e-24)
                p.act(rn, t2c, AF.Exp, scale=-0.5)
                p.ts(mkraw, mkraw, rn, ALU.mult)
                for fc in range(6):
                    p.transp(tpsf[:, 0:128], mkraw[:, fc * 128:(fc + 1) * 128], cm.ident[:])
                    p.copy(mkfm[:, fc * 128:(fc + 1) * 128], tpsf[:, 0:128], eng="scalar")
                for fc in range(6):
                    p.mm(ndps[:, 0:129], mkfm[:, fc * 128:(fc + 1) * 128],
                         nm[:, (fc * 16 + h) * 129:(fc * 16 + h + 1) * 129],
                         start=(fc == 0), stop=(fc == 5))
                p.ts(dcol, ndps[:, 128:129], 1e-8, ALU.add)
                p.recip(rd, dcol)
                p.ts(mhs, ndps[:, 0:128], rd, ALU.mult)
                p.transp(tpsf[:, 0:128], mhs, cm.ident[:])
                p.copy(mh_fm[:, h * 512 + tcw * 128: h * 512 + (tcw + 1) * 128],
                       tpsf[:, 0:128], eng="scalar")
        cs = slice(n * 512, (n + 1) * 512)
        for oc in range(NCHUNK):
            p.dma_cast(w1t.rearrange("p (c o) -> p c o", o=128),
                       wd["wgT"].rearrange("(c r) o -> r c o", r=128)[:, :, oc * 128:(oc + 1) * 128])
            for c in range(NCHUNK):
                p.mm(s2ps, w1t[:, c * 128:(c + 1) * 128],
                     pl.xn[:, c * T + n * 512:c * T + (n + 1) * 512],
                     start=(c == 0), stop=(c == NCHUNK - 1))
            p.act(gsb, s2ps, AF.Sigmoid, bias=biases[:, oc:oc + 1])
            p.tt(t1, gsb, mh_fm[:, oc * 512:(oc + 1) * 512], ALU.mult)
            p.tt(t1, t1, pl.xn[:, oc * T + n * 512:oc * T + (n + 1) * 512], ALU.add)
            p.dma(xres, x_d[oc * 128:(oc + 1) * 128, cs])
            p.tt(t1, t1, xres, ALU.add)
            p.dma(xout_d[oc * 128:(oc + 1) * 128, cs], t1)


# version 03
# speedup vs baseline: 1.0154x; 1.0154x over previous
import contextlib
import numpy as np
import ml_dtypes
import concourse.bass as bass
import concourse.mybir as mybir
from concourse.bass_utils import run_bass_kernel_spmd

F32 = mybir.dt.float32
BF16 = mybir.dt.bfloat16
LP16 = mybir.dt.float16
I32 = mybir.dt.int32
AF = mybir.ActivationFunctionType
ALU = mybir.AluOpType
AX = mybir.AxisListType

BF = ml_dtypes.bfloat16
NP16 = np.float16
DEBUG_STREAMS = False
_DBG = {}

D = 2048
T = 2048
NCHUNK = 16
NTCH = 16
POLL_CAP = 20_000_000
LN_EPS = 1e-5


class Prog:
    def __init__(self, name="prog"):
        self.nc = bass.Bass("TRN2", target_bir_lowering=False, debug=False)
        self.ops = []
        self.st = contextlib.ExitStack()
        self._uid = 0

    def uid(self, tag):
        self._uid += 1
        return f"{tag}_{self._uid}"

    def sbuf(self, tag, shape, dt):
        return self.st.enter_context(self.nc.sbuf_tensor(self.uid(tag), shape, dt))

    def psum(self, tag, shape, dt=F32):
        return self.st.enter_context(self.nc.psum_tensor(self.uid(tag), shape, dt))

    def dram(self, tag, shape, dt, addr_space="Local"):
        return self.nc.dram_tensor(self.uid(tag), shape, dt, kind="Internal",
                                   addr_space=addr_space).ap()

    def inp(self, name, shape, dt):
        return self.nc.dram_tensor(name, shape, dt, kind="ExternalInput").ap()

    def out(self, name, shape, dt):
        return self.nc.dram_tensor(name, shape, dt, kind="ExternalOutput").ap()

    def op(self, eng, fn, inc=1):
        import traceback
        frames = traceback.extract_stack()[:-2]
        site = ">".join(str(fr.lineno) for fr in frames if fr.filename.endswith("kernel.py"))
        self.ops.append((eng, inc, fn, site))

    def dma(self, out, in_, eng="sync", allow_slow=False):
        self.op(eng, lambda e, o=out, i=in_, a=allow_slow:
                e.dma_start(out=o, in_=i, allow_slow_non_contiguous=a), 16)

    def dma_cast(self, out, in_):
        self.op("gpsimd", lambda e, o=out, i=in_: e.dma_start(out=o, in_=i), 16)

    def mm(self, out, lhsT, rhs, start, stop):
        self.op("tensor", lambda e, o=out, l=lhsT, r=rhs, s=start, p=stop:
                e.matmul(o, l, r, start=s, stop=p))

    def transp(self, out, in_, ident):
        self.op("tensor", lambda e, o=out, i=in_, d=ident: e.transpose(o, i, d))

    def act(self, out, in_, func, bias=None, scale=None, accum_out=None):
        def fn(e, o=out, i=in_, f=func, b=bias, s=scale, a=accum_out):
            kw = {}
            if b is not None:
                kw["bias"] = b
            if s is not None:
                kw["scale"] = s
            if a is not None:
                kw["accum_out"] = a
            return e.activation(out=o, in_=i, func=f, **kw)
        self.op("scalar", fn)

    def tt(self, out, in0, in1, alu, eng="vector"):
        self.op(eng, lambda e, o=out, a=in0, b=in1, op=alu:
                e.tensor_tensor(out=o, in0=a, in1=b, op=op))

    def ts(self, out, in0, s1, op0, s2=None, op1=None, eng="vector"):
        if s2 is None:
            s2, op1 = 0.0, ALU.bypass
        self.op(eng, lambda e, o=out, i=in0, a=s1, oa=op0, b=s2, ob=op1:
                e.tensor_scalar(out=o, in0=i, scalar1=a, scalar2=b, op0=oa, op1=ob))

    def red(self, out, in_, alu, eng="vector"):
        self.op(eng, lambda e, o=out, i=in_, op=alu:
                e.tensor_reduce(out=o, in_=i, axis=AX.X, op=op))

    def recip(self, out, in_):
        self.op("vector", lambda e, o=out, i=in_: e.reciprocal(out=o, in_=i))

    def copy(self, out, in_, eng="vector"):
        if eng == "scalar":
            self.act(out, in_, AF.Copy)
        else:
            self.op(eng, lambda e, o=out, i=in_: e.tensor_copy(out=o, in_=i))

    def memset(self, ap, v, eng="vector"):
        self.op(eng, lambda e, a=ap, vv=v: e.memset(a, vv))

    def poll(self, flag_ap, nonce_sb_ap, eng="sync"):
        tag = self.uid("poll")

        def fn(e, fa=flag_ap, na=nonce_sb_ap, t=tag):
            with contextlib.ExitStack() as st:
                fr = st.enter_context(e.register(f"fr_{t}"))
                nr = st.enter_context(e.register(f"nr_{t}"))
                cr = st.enter_context(e.register(f"cr_{t}"))
                gt = st.enter_context(e.register(f"gt_{t}"))
                bud = st.enter_context(e.register(f"bud_{t}"))
                e.load(nr, na)
                e.reg_mov(bud, POLL_CAP)
                e.load(fr, fa)
                e.reg_alu(cr, fr, nr, ALU.not_equal)
                with e.While(cr):
                    e.reg_sub(bud, bud, 1)
                    e.load(fr, fa)
                    e.reg_alu(cr, fr, nr, ALU.not_equal)
                    e.reg_alu(gt, bud, 0, ALU.is_gt)
                    e.reg_alu(cr, cr, gt, ALU.bitwise_and)
            return e.nop()
        self.op(eng, fn)

    def pdma(self, out_even, out_odd, in_even, in_odd, eng="sync"):
        def fn(e, oe=out_even, oo=out_odd, ie=in_even, io=in_odd):
            par = self._parreg[e]
            with e.If_eq(par, 0):
                e.dma_start(out=oe, in_=ie).then_inc(self._gsem, 16)
            with e.Else():
                e.dma_start(out=oo, in_=io).then_inc(self._gsem, 16)
            return None
        self.ops.append((eng, 16, ("raw", fn), "pdma"))

    def finalize(self, limit=None):
        if limit is not None:
            self.ops = self.ops[:limit]
        nc = self.nc
        sem = self.st.enter_context(nc.semaphore("gsem"))
        self._gsem = sem
        block = self.st.enter_context(nc.Block())

        sched = []
        c = 0
        i = 0
        ops = self.ops

        def _runnable(op):
            eng, inc, fn = op[0], op[1], op[2]
            if isinstance(fn, tuple):
                return None
            if eng == "tensor":
                return "tensor"
            if inc == 16:
                return "dma:" + eng
            return None

        while i < len(ops):
            eng, inc, fn, site = ops[i]
            kind = _runnable(ops[i])
            if kind is not None:
                j = i
                run = []
                tot = 0
                while j < len(ops) and _runnable(ops[j]) == kind:
                    run.append(ops[j])
                    tot += ops[j][1]
                    j += 1
                if kind == "tensor":
                    for k, (eg, ic, f, st_) in enumerate(run):
                        w = c if k == 0 else None
                        ii = tot if k == len(run) - 1 else 0
                        sched.append((eg, f, w, ii, st_))
                else:
                    for k, (eg, ic, f, st_) in enumerate(run):
                        w = c if k == 0 else None
                        sched.append((eg, f, w, ic, st_))
                c += tot
                i = j
            else:
                sched.append((eng, fn, c, inc, site))
                c += inc
                i += 1
        total = c
        by_eng = {}
        for eng, fn, wait, inc, site in sched:
            by_eng.setdefault(eng, []).append((fn, wait, inc, site))

        self._parreg = {}

        def emit_for(engobj, items):
            pr = engobj.register(self.uid("parity")).__enter__()
            pid = engobj.partition_id()
            engobj.reg_alu(pr, pid, 1, ALU.bitwise_and)
            self._parreg[engobj] = pr
            for fn, wait, inc, site in items:
                if wait is not None and wait > 0:
                    engobj.wait_ge(sem, wait)
                try:
                    if isinstance(fn, tuple) and fn[0] == "raw":
                        fn[1](engobj)
                    else:
                        instr = fn(engobj)
                        if inc:
                            instr.then_inc(sem, inc)
                except Exception as ex:
                    raise RuntimeError(f"op emitted at {site} failed: {ex}") from ex

        @block.sync
        def _(s):
            emit_for(s, by_eng.get("sync", []))
            s.wait_ge(sem, total)

        @block.gpsimd
        def _(g):
            emit_for(g, by_eng.get("gpsimd", []))

        @block.tensor
        def _(t):
            emit_for(t, by_eng.get("tensor", []))

        @block.vector
        def _(v):
            emit_for(v, by_eng.get("vector", []))

        @block.scalar
        def _(s):
            emit_for(s, by_eng.get("scalar", []))

        self.st.close()
        return nc


R_ONES128 = 0
SC_EPS = 128
SC_INV2048 = 129
SC_NEG1E30 = 130
SC_LAEPS = 131
SC_ONE = 132
SC_L2EPS = 133
SC_EPS24 = 134
SC_SQN = 135
CROW_COLS = 256
C_ONES = 0


def host_consts():
    consts = np.zeros((128, 16), np.float32)
    consts[:, C_ONES] = 1.0
    consts[:, 1] = 1e-24
    crow = np.zeros((1, CROW_COLS), np.float32)
    crow[0, 0:128] = 1.0
    crow[0, SC_EPS] = LN_EPS
    crow[0, SC_INV2048] = 1.0 / 2048.0
    crow[0, SC_NEG1E30] = -1e30
    crow[0, SC_LAEPS] = 1e-8
    crow[0, SC_ONE] = 1.0
    crow[0, SC_L2EPS] = 1e-12
    crow[0, SC_EPS24] = 1e-24
    crow[0, SC_SQN] = 256.0 / 2048.0
    return consts, crow


class Common:
    def __init__(self, p: Prog):
        self.p = p
        self.consts_d = p.inp("consts", [128, 16], F32)
        self.crow_d = p.inp("crow", [1, CROW_COLS], F32)
        self.ident_d = p.inp("ident_bf", [128, 128], LP16)
        self.identf_d = p.inp("ident_f32", [128, 128], F32)
        self.nonce_d = p.inp("nonce", [1, 4], I32)

        self.consts = p.sbuf("consts", [128, 16], F32)
        self.crow = p.sbuf("crow", [1, CROW_COLS], F32)
        self.onesb = p.sbuf("onesb", [1, 128], LP16)
        self.negonesb = p.sbuf("negonesb", [1, 128], LP16)
        self.onescol_b = p.sbuf("onescol_b", [128, 1], LP16)
        self.onescol_bb = p.sbuf("onescol_bb", [128, 1], BF16)
        self.ident = p.sbuf("ident", [128, 128], LP16)
        self.identf = p.sbuf("identf", [128, 128], F32)
        self.nonce = p.sbuf("nonce", [1, 4], I32)

        p.dma(self.consts[:], self.consts_d)
        p.dma(self.crow[:], self.crow_d)
        p.dma(self.ident[:], self.ident_d)
        p.dma(self.identf[:], self.identf_d)
        p.dma(self.nonce[:], self.nonce_d)
        p.nc.const_aps.register(F32, 1e-24, self.consts[:, 1:2])
        p.copy(self.onesb[:], self.crow[0:1, 0:128])
        p.ts(self.negonesb[:], self.crow[0:1, 0:128], -1.0, ALU.mult)
        p.copy(self.onescol_b[:], self.consts[:, C_ONES:C_ONES + 1])
        p.copy(self.onescol_bb[:], self.consts[:, C_ONES:C_ONES + 1])

    def onesf(self, n=128):
        return self.crow[0:1, 0:n]

    def onescol(self):
        return self.consts[:, C_ONES:C_ONES + 1]

    def sc(self, col):
        return self.crow[0:1, col:col + 1]


class Pools:

    def __init__(self, p: Prog):
        self.xn = p.sbuf("xn_pool", [128, NCHUNK * T], LP16)
        self.w = p.sbuf("w_pool", [128, 34816], LP16)
        self.ab = p.sbuf("a_pool", [128, 16384], LP16)
        self.f = p.sbuf("f_pool", [128, 3072], F32)
        self.rows = p.sbuf("rows_pool", [128, 4096], F32)
        self.pp0 = p.psum("pp0", [128, 2048])
        self.pp1 = p.psum("pp1", [128, 2048])

    def row(self, base, off, n):
        return self.rows[base:base + 1, off:off + n]


def emit_layernorm(p: Prog, cm: Common, pl: Pools, x_d, xn_dram=None):
    xc = pl.f[:, 0:1024]
    xw = pl.f[:, 1024:1536]
    tmp = pl.f[:, 1536:2048]
    xb = pl.ab[:, 0:1024]
    x2 = pl.ab[:, 1024:2048]
    sums = pl.pp0[0:1, 0:1024]
    sumsq = pl.pp0[0:1, 1024:2048]
    bc = pl.pp1[:, 0:1024]
    mrow = pl.row(0, 0, T)
    rrow = pl.row(0, 2048, T)
    srow = pl.row(32, 0, T)
    s2row = pl.row(32, 2048, T)

    for h in range(2):
        hs = slice(h * 1024, (h + 1) * 1024)
        for c in range(NCHUNK):
            p.dma(xc, x_d[c * 128:(c + 1) * 128, hs])
            p.copy(xb, xc)
            p.act(x2, xc, AF.Square, scale=0.0625)
            for n in range(2):
                cs = slice(n * 512, (n + 1) * 512)
                p.mm(sums[0:1, cs], cm.onescol_b[:], xb[:, cs], start=(c == 0), stop=(c == NCHUNK - 1))
                p.mm(sumsq[0:1, cs], cm.onescol_b[:], x2[:, cs], start=(c == 0), stop=(c == NCHUNK - 1))
        mr = mrow[0:1, hs]
        rr = rrow[0:1, hs]
        sr = srow[0:1, hs]
        s2r = s2row[0:1, hs]
        p.act(mr, sums, AF.Copy, scale=cm.sc(SC_INV2048))
        p.act(sr, sumsq, AF.Copy, scale=cm.sc(SC_SQN))
        p.tt(s2r, mr, mr, ALU.mult)
        p.tt(sr, sr, s2r, ALU.subtract)
        p.act(s2r, sr, AF.Ln, bias=cm.sc(SC_EPS))
        p.act(rr, s2r, AF.Exp, scale=-0.5)
        for n in range(2):
            gn = h * 1024 + n * 512
            cs = slice(gn, gn + 512)
            p.mm(bc[:, 0:512], cm.onesf(), mrow[0:1, cs], start=True, stop=True)
            p.mm(bc[:, 512:1024], cm.onesf(), rrow[0:1, cs], start=True, stop=True)
            for c in range(NCHUNK):
                p.dma(xw, x_d[c * 128:(c + 1) * 128, cs])
                p.tt(tmp, xw, bc[:, 0:512], ALU.subtract)
                if xn_dram is None:
                    p.tt(pl.xn[:, c * T + gn: c * T + gn + 512], tmp, bc[:, 512:1024], ALU.mult)
                else:
                    p.tt(xb[:, 0:512], tmp, bc[:, 512:1024], ALU.mult)
                    p.dma(xn_dram[c * 128:(c + 1) * 128, cs], xb[:, 0:512])


def load_w(p, w_d, dst, cin, ocols):
    p.dma_cast(dst, w_d[cin * 128:(cin + 1) * 128, ocols])


def emit_l0ffn(p: Prog, cm: Common, pl: Pools, x_d, xout_d, wd):
    wsb = pl.w[:, 0:32768]
    asb = pl.w[:, 32768:34816]
    bsb = pl.ab[:, 1152:3200]
    gsb = pl.ab[:, 0:128]
    tall = pl.ab[:, 128:640]
    tsc = pl.ab[:, 640:1152]
    xres = pl.f[:, 0:512]
    osb = pl.f[:, 512:1024]
    e16f = pl.f[0:8, 1024:1152]
    cw_f = pl.rows[0:8, 0:2048]

    for c in range(NCHUNK):
        load_w(p, wd["wT"], wsb[:, c * D:(c + 1) * D], c, slice(0, D))
        load_w(p, wd["aT"], asb[:, c * 128:(c + 1) * 128], c, slice(0, 128))
        load_w(p, wd["gwT"], gsb[:, c * 8:(c + 1) * 8], c, slice(0, 8))
    p.dma_cast(bsb[:, 0:D], wd["bT"])
    p.dma(e16f, wd["e16"])

    scps = pl.pp0[:, 0:8]
    trps = pl.pp1[0:8, 0:128]
    m1 = pl.f[:, 2048:2049]
    nm1 = pl.f[:, 2049:2050]
    m2 = pl.f[:, 2050:2051]
    ll = pl.f[:, 2051:2052]
    rl = pl.f[:, 2052:2053]
    ex = pl.f[:, 2056:2064]
    msk = pl.f[:, 2064:2072]
    s2 = pl.f[:, 2072:2080]
    ww = pl.f[:, 2080:2088]
    comb = pl.f[:, 2088:2096]

    for tc in range(NTCH):
        tcs = slice(tc * 128, (tc + 1) * 128)
        for c in range(NCHUNK):
            p.mm(scps, pl.xn[:, c * T + tc * 128: c * T + (tc + 1) * 128],
                 gsb[:, c * 8:(c + 1) * 8], start=(c == 0), stop=(c == NCHUNK - 1))
        p.red(m1, scps, ALU.max)
        p.ts(nm1, m1, -1.0, ALU.mult)
        p.act(ex, scps, AF.Exp, bias=nm1)
        p.ts(msk, scps, m1, ALU.is_ge)
        p.ts(s2, msk, -1e30, ALU.mult)
        p.tt(s2, scps, s2, ALU.add)
        p.red(m2, s2, ALU.max)
        p.ts(msk, scps, m2, ALU.is_ge)
        p.tt(ww, ex, msk, ALU.mult)
        p.red(ll, ww, ALU.add)
        p.recip(rl, ll)
        p.ts(comb, ww, rl, ALU.mult)
        p.mm(trps, comb, cm.identf[:], start=True, stop=True)
        p.copy(cw_f[:, tcs], trps)

    tps = pl.pp0[:, 512:1024]
    cwx = pl.pp1[:, 512:1024]
    ops_ = pl.pp0[:, 1024:1536]
    for n in range(4):
        cs = slice(n * 512, (n + 1) * 512)
        for c in range(NCHUNK):
            p.mm(tps, asb[:, c * 128:(c + 1) * 128],
                 pl.xn[:, c * T + n * 512: c * T + (n + 1) * 512],
                 start=(c == 0), stop=(c == NCHUNK - 1))
        p.copy(tall, tps, eng="scalar")
        p.mm(cwx, e16f, cw_f[:, cs], start=True, stop=True)
        p.tt(tsc, tall, cwx, ALU.mult)
        for oc in range(NCHUNK):
            for c in range(NCHUNK):
                p.mm(ops_, wsb[:, c * D + oc * 128: c * D + (oc + 1) * 128],
                     pl.xn[:, c * T + n * 512: c * T + (n + 1) * 512],
                     start=(c == 0), stop=False)
            p.mm(ops_, bsb[:, oc * 128:(oc + 1) * 128], tsc, start=False, stop=True)
            p.dma(xres, x_d[oc * 128:(oc + 1) * 128, cs])
            p.tt(osb, ops_, xres, ALU.add)
            p.dma(xout_d[oc * 128:(oc + 1) * 128, cs], osb)


def _f(a):
    return np.ascontiguousarray(np.asarray(a, np.float32))


def prep_l0ffn(inputs):
    w = {}
    w["wT"] = _f(np.asarray(inputs["L0_ffn_weight"]).T)
    w["gwT"] = _f(np.asarray(inputs["L0_ffn_gate_w"]).T)
    A = _f(inputs["L0_ffn_A_w"])
    w["aT"] = _f(A.transpose(2, 0, 1).reshape(D, 128))
    Bw = _f(inputs["L0_ffn_B_w"])
    w["bT"] = _f(Bw.transpose(0, 2, 1).reshape(128, D))
    e16 = np.zeros((8, 128), np.float32)
    for e in range(8):
        e16[e, 16 * e:16 * (e + 1)] = 16.0
    w["e16"] = e16
    return w


def host_ident():
    return np.eye(128, dtype=np.float32).astype(NP16), np.eye(128, dtype=np.float32)


def shard_tokens(x_b):
    blocks = np.asarray(x_b, np.float32).reshape(32, 128, D)
    ev = blocks[0::2].reshape(T, D)
    od = blocks[1::2].reshape(T, D)
    return _f(ev.T), _f(od.T)


def unshard_tokens(ev_T, od_T):
    ev = np.asarray(ev_T).T.reshape(16, 128, D)
    od = np.asarray(od_T).T.reshape(16, 128, D)
    out = np.empty((32, 128, D), np.float32)
    out[0::2] = ev
    out[1::2] = od
    return out.reshape(4096, D)



class Shr:
    FL_HELLO = 0
    FL_L0AA = 2
    FL_L0AB = 4
    FL_L1AA = 6
    FL_L1AB = 8

    def __init__(self, p: Prog):
        self.p = p
        self.flags = p.dram("shr_flags", [1, 64], I32, addr_space="Shared")
        self.zz = p.sbuf("shr_zz", [1, 4], I32)
        p.memset(self.zz[:], 0)

    def hello(self, cm):
        p = self.p
        for i in range(2, 16, 2):
            p.pdma(self.flags[0:1, i:i + 1], self.flags[0:1, i + 1:i + 2],
                   self.zz[0:1, 0:1], self.zz[0:1, 0:1])
        p.pdma(self.flags[0:1, 0:1], self.flags[0:1, 1:2],
               cm.nonce[0:1, 0:1], cm.nonce[0:1, 0:1])
        p.poll(self.flags[0:1, 0:1], cm.nonce[0:1, 0:1])
        p.poll(self.flags[0:1, 1:2], cm.nonce[0:1, 0:1])

    def set_flag(self, cm, idx):
        self.p.pdma(self.flags[0:1, idx:idx + 1], self.flags[0:1, idx + 1:idx + 2],
                    cm.nonce[0:1, 0:1], cm.nonce[0:1, 0:1])

    def wait_flag(self, cm, idx):
        self.p.poll(self.flags[0:1, idx:idx + 1], cm.nonce[0:1, 0:1])
        self.p.poll(self.flags[0:1, idx + 1:idx + 2], cm.nonce[0:1, 0:1])


def emit_attn_core(p: Prog, cm: Common, pl: Pools, tag, qdr, xattn, ksh, vsh,
                   nheads, hd, att_scale, causal, mode,
                   negC_bf=None, amask_sb=None, pv_bf=False):
    aug = (hd == 64)
    vtw = hd + 1 if aug else hd
    ksb = pl.ab[:, 0:4096]
    vsb = pl.ab[:, 4096:4096 + 2 * 16 * vtw]
    pb1 = pl.ab[:, 8448:8960]
    pb2 = pl.ab[:, 8960:9472]
    if pv_bf:
        vsb = vsb.bitcast(BF16)
        pb1 = pb1.bitcast(BF16)
        pb2 = pb2.bitcast(BF16)
        if amask_sb is not None:
            amask_sb = amask_sb.bitcast(BF16)
    vsbv = vsb.rearrange("p (t c) -> p t c", c=vtw)
    onescol_pv = cm.onescol_bb[:] if pv_bf else cm.onescol_b[:]
    qsb = pl.ab[:, 9472:9984]
    frow = pl.ab[0:1, 9984:10496]
    mbf = pl.ab[:, 10496:10497]
    osb = pl.f[:, 0:512]
    rl = pl.row(0, 512, 512)
    macc = pl.f[:, 513:514]
    mt = pl.f[:, 514:515]
    ones512 = pl.row(0, 0, 512)

    for ql in range(4):
        i_max = 4 * ql + 3
        nkb = (i_max + 1) if causal else 16
        for h in range(nheads):
            hc = (h * hd) // 128
            ho = (h * hd) % 128
            qh = qsb[ho:ho + hd, 0:512]
            p.dma(qh, qdr[h * hd:(h + 1) * hd, ql * 512:(ql + 1) * 512])
            for s in range(2):
                kr = s * nheads * hd + h * hd
                p.dma(ksb[ho:ho + hd, s * 2048: s * 2048 + nkb * 128],
                      ksh[kr:kr + hd, 0:nkb * 128])
            for s in range(2):
                dst = vsbv[:, s * 16:s * 16 + nkb, 0:hd]
                src = vsh[s * 2048: s * 2048 + nkb * 128, h * hd:(h + 1) * hd] \
                    .rearrange("(kb p) c -> p kb c", p=128)
                p.dma(dst, src)
                if aug:
                    p.memset(vsbv[:, s * 16:s * 16 + nkb, hd:hd + 1], 1.0)
            if mode == "bound":
                p.ts(frow, ones512, negC_bf[0:1, h:h + 1], ALU.mult)
            else:
                tps = pl.pp0[0:1, 1536:2048]
                for qb in range(4):
                    qhb = qsb[ho:ho + hd, qb * 128:(qb + 1) * 128]
                    first = True
                    for s in range(2):
                        for jg in range(0, nkb, 4):
                            w = min(4, nkb - jg) * 128
                            S1 = pl.pp0[:, 1024:1536]
                            p.mm(S1[:, 0:w], qhb,
                                 ksb[ho:ho + hd, s * 2048 + jg * 128: s * 2048 + jg * 128 + w],
                                 start=True, stop=True)
                            p.red(mt, S1[:, 0:w], ALU.max)
                            if first:
                                p.copy(macc, mt)
                                first = False
                            else:
                                p.tt(macc, macc, mt, ALU.max)
                    p.ts(mbf, macc, -1.0, ALU.mult)
                    p.mm(tps[0:1, qb * 128:(qb + 1) * 128], mbf, cm.ident[:],
                         start=True, stop=True)
                p.copy(frow, tps, eng="scalar")
            if aug:
                O = pl.pp1[0:65, 0:512]
            else:
                O = pl.pp1[0:128, 0:512]
                lps = pl.pp1[0:1, 1536:2048]
            first = True
            for s in range(2):
                for j in range(nkb):
                    Spp = pl.pp0[:, 0:512] if (j % 2 == 0) else pl.pp0[:, 512:1024]
                    p.mm(Spp, ksb[ho:ho + hd, s * 2048 + j * 128: s * 2048 + (j + 1) * 128],
                         qh, start=True, stop=False)
                    p.mm(Spp, cm.onesb[:], frow, start=False, stop=True)
                    p.act(pb1, Spp, AF.Exp, scale=att_scale)
                    pv_in = pb1
                    if causal and j >= i_max - 3:
                        o = j - (i_max - 3)
                        p.tt(pb2, pb1, amask_sb[:, (o * 2 + s) * 512:(o * 2 + s + 1) * 512],
                             ALU.mult)
                        pv_in = pb2
                    p.mm(O, vsb[:, (s * 16 + j) * vtw:(s * 16 + j) * vtw + vtw], pv_in,
                         start=first, stop=(aug and s == 1 and j == nkb - 1))
                    if not aug:
                        p.mm(lps, onescol_pv, pv_in,
                             start=first, stop=(s == 1 and j == nkb - 1))
                    first = False
            if aug:
                p.recip(rl, O[64:65, :])
                R = pl.pp1[0:64, 512:1024]
                p.mm(R, cm.onesf(64), rl, start=True, stop=True)
                p.copy(osb[0:64, :], O[0:64, :], eng="scalar")
                p.tt(xattn[ho:ho + 64, hc * T + ql * 512: hc * T + (ql + 1) * 512],
                     osb[0:64, :], R, ALU.mult)
            else:
                p.recip(rl, lps)
                R = pl.pp1[0:128, 512:1024]
                p.mm(R, cm.onesf(), rl, start=True, stop=True)
                p.copy(osb[0:128, :], O[0:128, :], eng="scalar")
                p.tt(xattn[:, hc * T + ql * 512: hc * T + (ql + 1) * 512],
                     osb[0:128, :], R, ALU.mult)


def emit_l0_attn_a(p: Prog, cm: Common, pl: Pools, shr: Shr, x_d, xout_d, wd):
    tag = "l0aa"
    amask_sb = pl.ab[:, 10752:14848]
    p.dma(amask_sb, wd["amask"])

    w2 = pl.w[:, 0:16384]
    w1t = pl.w[:, 16384:18432]
    qdr = p.dram(tag + "_q", [1024, 2048], LP16)
    xattn = pl.xn[:, 0:16384]
    xP = pl.ab[:, 0:8192]
    ev = pl.ab[:, 8192:8704]
    sqb = pl.ab[:, 8704:9216]
    browb = pl.ab[0:1, 9216:10240]
    biases = pl.f[:, 2100:2116]
    b2c = pl.f[:, 2116:2124]
    s1ps = pl.pp0[:, 0:512]
    s2ps = pl.pp0[:, 512:1024]
    ssps = pl.pp1[0:1, 0:1024]

    ksh = p.dram(tag + "_ksh", [2 * 1024, 2048], LP16, addr_space="Shared")
    vsh = p.dram(tag + "_vsh", [2 * 2048, 1024], LP16, addr_space="Shared")
    stsh = p.dram(tag + "_st", [2, 16], F32, addr_space="Shared")

    kssmax = pl.row(0, 1024, 16)
    qssmax = pl.row(0, 1040, 16)
    tmps = pl.row(0, 1056, 16)
    tmp1 = pl.row(0, 1072, 1)
    negC = pl.row(0, 1088, 16)
    kpart = pl.row(0, 1104, 16)
    tmps2 = pl.row(0, 1120, 16)
    ones512 = pl.row(0, 0, 512)
    p.memset(ones512, 1.0)
    p.memset(kssmax, 0.0)
    p.memset(qssmax, 0.0)

    def load_bcols(b_d, dst, n):
        bt = b_d.rearrange("(c r) -> c r", r=128)
        for c in range(n):
            p.dma(dst[:, c:c + 1], bt[c:c + 1, :].rearrange("a r -> r a"))

    def stage1(chain, n):
        w1_d = wd[chain + "1T"]
        for oc in range(NCHUNK):
            p.dma_cast(w1t.rearrange("p (c o) -> p c o", o=128),
                       w1_d.rearrange("(c r) o -> r c o", r=128)[:, :, oc * 128:(oc + 1) * 128])
            for c in range(NCHUNK):
                p.mm(s1ps, w1t[:, c * 128:(c + 1) * 128],
                     pl.xn[:, c * T + n * 512:c * T + (n + 1) * 512],
                     start=(c == 0), stop=(c == NCHUNK - 1))
            p.act(xP[:, oc * 512:(oc + 1) * 512], s1ps, AF.Identity,
                  bias=biases[:, oc:oc + 1])

    def stats_from_evac(dstmax, kc):
        p.act(sqb, ev[:, 0:512], AF.Square)
        p.mm(ssps[0:1, 0:512], cm.onescol_b[0:64, 0:1], sqb[0:64, :], start=True, stop=True)
        p.mm(ssps[0:1, 512:1024], cm.onescol_b[64:128, 0:1], sqb[64:128, :], start=True, stop=True)
        for hh in range(2):
            h = kc * 2 + hh
            p.red(tmp1, ssps[0:1, hh * 512:(hh + 1) * 512], ALU.max)
            p.tt(dstmax[0:1, h:h + 1], dstmax[0:1, h:h + 1], tmp1, ALU.max)

    load_bcols(wd["bv"], biases, 16)
    p.dma_cast(browb, wd["bvl"].unsqueeze(0))
    for c in range(NCHUNK):
        load_w(p, wd["v2T"], w2[:, c * 1024:(c + 1) * 1024], c, slice(0, 1024))
    for n in range(4):
        stage1("v", n)
        for tb in range(4):
            for half in range(2):
                ocs = slice(half * 512, (half + 1) * 512)
                for c in range(NCHUNK):
                    p.mm(s2ps, xP[:, c * 512 + tb * 128: c * 512 + (tb + 1) * 128],
                         w2[:, c * 1024 + half * 512: c * 1024 + half * 512 + 512],
                         start=(c == 0), stop=False)
                p.mm(s2ps, cm.onesb[:], browb[0:1, ocs], start=False, stop=True)
                p.copy(ev[:, 0:512], s2ps, eng="scalar")
                trow = n * 512 + tb * 128
                p.pdma(vsh[trow:trow + 128, ocs], vsh[2048 + trow:2048 + trow + 128, ocs],
                       ev[:, 0:512], ev[:, 0:512])

    load_bcols(wd["bk"], biases, 16)
    load_bcols(wd["bkl"], b2c, 8)
    for c in range(NCHUNK):
        load_w(p, wd["k2T"], w2[:, c * 1024:(c + 1) * 1024], c, slice(0, 1024))
    for n in range(4):
        stage1("k", n)
        for kc in range(8):
            for c in range(NCHUNK):
                p.mm(s2ps, w2[:, c * 1024 + kc * 128: c * 1024 + (kc + 1) * 128],
                     xP[:, c * 512:(c + 1) * 512], start=(c == 0), stop=(c == NCHUNK - 1))
            p.act(ev[:, 0:512], s2ps, AF.Identity, bias=b2c[:, kc:kc + 1])
            krow = kc * 128
            p.pdma(ksh[krow:krow + 128, n * 512:(n + 1) * 512],
                   ksh[1024 + krow:1024 + krow + 128, n * 512:(n + 1) * 512],
                   ev[:, 0:512], ev[:, 0:512])
            stats_from_evac(kssmax, kc)
    p.pdma(stsh[0:1, :], stsh[1:2, :], kssmax, kssmax)
    shr.set_flag(cm, Shr.FL_L0AA)

    load_bcols(wd["bq"], biases, 16)
    load_bcols(wd["bql"], b2c, 8)
    for c in range(NCHUNK):
        load_w(p, wd["q2T"], w2[:, c * 1024:(c + 1) * 1024], c, slice(0, 1024))
    for n in range(4):
        stage1("q", n)
        for kc in range(8):
            for c in range(NCHUNK):
                p.mm(s2ps, w2[:, c * 1024 + kc * 128: c * 1024 + (kc + 1) * 128],
                     xP[:, c * 512:(c + 1) * 512], start=(c == 0), stop=(c == NCHUNK - 1))
            p.act(ev[:, 0:512], s2ps, AF.Identity, bias=b2c[:, kc:kc + 1])
            p.dma(qdr[kc * 128:(kc + 1) * 128, n * 512:(n + 1) * 512], ev[:, 0:512])
            stats_from_evac(qssmax, kc)

    shr.wait_flag(cm, Shr.FL_L0AA)
    p.pdma(kpart, kpart, stsh[1:2, :], stsh[0:1, :])
    p.tt(kssmax, kssmax, kpart, ALU.max)
    p.tt(tmps, kssmax, qssmax, ALU.mult)
    p.act(tmps2, tmps, AF.Ln, bias=cm.sc(SC_L2EPS))
    p.act(tmps, tmps2, AF.Exp, scale=0.5)
    p.ts(negC, tmps, -1.0, ALU.mult)

    emit_attn_core(p, cm, pl, tag, qdr, xattn, ksh, vsh,
                   nheads=16, hd=64, att_scale=0.125, causal=True,
                   mode="bound", negC_bf=negC, amask_sb=amask_sb)

    for c in range(8):
        load_w(p, wd["oT"], w2[:, c * 2048:(c + 1) * 2048], c, slice(0, 2048))
    load_bcols(wd["bo"], biases, 16)
    xres = pl.f[:, 1024:1536]
    osb2 = pl.f[:, 1536:2048]
    for n in range(4):
        cs = slice(n * 512, (n + 1) * 512)
        for oc in range(NCHUNK):
            for c in range(8):
                p.mm(s2ps, w2[:, c * 2048 + oc * 128: c * 2048 + (oc + 1) * 128],
                     xattn[:, c * T + n * 512: c * T + (n + 1) * 512],
                     start=(c == 0), stop=(c == 7))
            p.act(osb2, s2ps, AF.Identity, bias=biases[:, oc:oc + 1])
            p.dma(xres, x_d[oc * 128:(oc + 1) * 128, cs])
            p.tt(osb2, osb2, xres, ALU.add)
            p.dma(xout_d[oc * 128:(oc + 1) * 128, cs], osb2)


def prep_l0aa(inputs, parity):
    w = {}
    w["v1T"] = _f(inputs["L0_attn_a_wv"])
    w["k1T"] = _f(inputs["L0_attn_a_wk"])
    w["q1T"] = _f(inputs["L0_attn_a_wq"])
    w["v2T"] = _f(inputs["L0_attn_a_wvl"])
    w["k2T"] = _f(inputs["L0_attn_a_wkl"])
    w["q2T"] = _f(inputs["L0_attn_a_wql"])
    w["oT"] = _f(inputs["L0_attn_a_wo"])
    w["bv"] = _f(inputs["L0_attn_a_bv"])
    w["bk"] = _f(inputs["L0_attn_a_bk"])
    w["bq"] = _f(inputs["L0_attn_a_bq"])
    w["bvl"] = _f(inputs["L0_attn_a_bvl"])
    w["bkl"] = _f(inputs["L0_attn_a_bkl"])
    w["bql"] = _f(inputs["L0_attn_a_bql"])
    w["bo"] = _f(inputs["L0_attn_a_bo"])
    w["amask"] = build_amask(parity)
    return w


def build_amask(par, dtype=None):
    if dtype is None:
        dtype = NP16
    m = np.zeros((128, 8, 512), np.float32)
    for o in range(4):
        for s in range(2):
            for qq in range(4):
                rel = o - qq
                blk = np.zeros((128, 128), np.float32)
                if s == par:
                    if rel < 0:
                        blk[:] = 1.0
                    elif rel == 0:
                        blk = np.tril(np.ones((128, 128), np.float32)).T
                else:
                    if par == 0:
                        blk[:] = 1.0 if rel < 0 else 0.0
                    else:
                        blk[:] = 1.0 if rel <= 0 else 0.0
                m[:, o * 2 + s, qq * 128:(qq + 1) * 128] = blk
    out = np.ascontiguousarray(m.reshape(128, 8 * 512)).astype(dtype)
    if dtype is not NP16:
        out = out.view(NP16)
    return out

def emit_attn_std(p: Prog, cm: Common, pl: Pools, shr: Shr, x_d, xout_d, wd, cfg):
    tag = cfg["tag"]
    nheads, hd = cfg["nheads"], cfg["hd"]
    AH = nheads * hd
    AHC = AH // 128
    causal, mode = cfg["causal"], cfg["mode"]

    wcur = pl.w[:, 0:32768]
    ev = pl.ab[:, 8192:8704]
    ev2 = pl.ab[:, 8704:9216]
    sqb = pl.ab[:, 9216:9728]
    rop1 = pl.ab[:, 9728:10240]
    biases = pl.f[:, 2100:2116]
    browb = pl.ab[0:1, 4224:4224 + AH]
    s2ps = pl.pp0[:, 512:1024]
    rqp = pl.pp1[:, 1024:1536]
    ssps = pl.pp1[0:1, 0:1024]
    pv_bf = cfg.get("pv_bf", False)
    pvdt = BF16 if pv_bf else LP16
    qdr = p.dram(tag + "_q", [AH, 2048], LP16)
    ksh = p.dram(tag + "_ksh", [2 * AH, 2048], LP16, addr_space="Shared")
    vsh = p.dram(tag + "_vsh", [2 * 2048, AH], pvdt, addr_space="Shared")
    stsh = p.dram(tag + "_st", [2, nheads], F32, addr_space="Shared")
    _DBG.update(qdr=qdr, ksh=ksh, vsh=vsh)

    amask_sb = None
    if causal:
        amask_sb = pl.ab[:, 10752:14848]
        p.dma(amask_sb, wd["amask"])
    if cfg["rope"]:
        costab = pl.ab[:, 0:2048]
        sintab = pl.ab[:, 2048:4096]
        rT = pl.ab[:, 6400:6528]
        p.dma(costab, wd["costab"])
        p.dma(sintab, wd["sintab"])
        p.dma(rT, wd["rT"])

    kssmax = pl.row(0, 1024, nheads)
    qssmax = pl.row(0, 1024 + 32, nheads)
    tmps = pl.row(0, 1024 + 64, nheads)
    tmps2 = pl.row(0, 1024 + 96, nheads)
    tmp1 = pl.row(0, 1024 + 128, 1)
    negC = pl.row(0, 1024 + 160, nheads)
    kpart = pl.row(0, 1024 + 192, nheads)
    ones512 = pl.row(0, 0, 512)
    p.memset(ones512, 1.0)
    p.memset(kssmax, 0.0)
    p.memset(qssmax, 0.0)

    def load_bcols(b_d, dst, n):
        bt = b_d.rearrange("(c r) -> c r", r=128)
        for c in range(n):
            p.dma(dst[:, c:c + 1], bt[c:c + 1, :].rearrange("a r -> r a"))

    def load_wcur(w_d, ncols):
        for c in range(NCHUNK):
            load_w(p, w_d, wcur[:, c * ncols:(c + 1) * ncols], c, slice(0, ncols))

    def rope_apply(n):
        cs = slice(n * 512, (n + 1) * 512)
        p.mm(rqp[:, 0:512], rT, ev, start=True, stop=True)
        p.tt(rop1, ev, costab[:, cs], ALU.mult)
        p.tt(ev2, rqp[:, 0:512], sintab[:, cs], ALU.mult)
        p.tt(ev2, rop1, ev2, ALU.add)

    def stats_from_evac(dstmax, kc):
        p.act(sqb, ev2 if cfg["rope"] else ev, AF.Square)
        if hd == 64:
            p.mm(ssps[0:1, 0:512], cm.onescol_b[0:64, 0:1], sqb[0:64, :], start=True, stop=True)
            p.mm(ssps[0:1, 512:1024], cm.onescol_b[64:128, 0:1], sqb[64:128, :], start=True, stop=True)
            pairs = [(kc * 2, 0), (kc * 2 + 1, 512)]
        else:
            p.mm(ssps[0:1, 0:512], cm.onescol_b[:], sqb, start=True, stop=True)
            pairs = [(kc, 0)]
        for h, off in pairs:
            p.red(tmp1, ssps[0:1, off:off + 512], ALU.max)
            p.tt(dstmax[0:1, h:h + 1], dstmax[0:1, h:h + 1], tmp1, ALU.max)

    evv = ev.bitcast(BF16) if pv_bf else ev
    load_wcur(wd["vT"], 2048)
    p.dma_cast(browb, wd["bv"].unsqueeze(0))
    for n in range(4):
        for tb in range(4):
            for half in range(AH // 512):
                ocs = slice(half * 512, (half + 1) * 512)
                for c in range(NCHUNK):
                    p.mm(s2ps, pl.xn[:, c * T + n * 512 + tb * 128: c * T + n * 512 + (tb + 1) * 128],
                         wcur[:, c * 2048 + half * 512: c * 2048 + half * 512 + 512],
                         start=(c == 0), stop=False)
                p.mm(s2ps, cm.onesb[:], browb[0:1, ocs], start=False, stop=True)
                p.copy(evv[:, 0:512], s2ps, eng="scalar")
                trow = n * 512 + tb * 128
                p.pdma(vsh[trow:trow + 128, ocs], vsh[2048 + trow:2048 + trow + 128, ocs],
                       evv[:, 0:512], evv[:, 0:512])

    load_wcur(wd["kT"], 2048)
    load_bcols(wd["bk"], biases, 16)
    for n in range(4):
        for kc in range(AHC):
            for c in range(NCHUNK):
                p.mm(s2ps, wcur[:, c * 2048 + kc * 128: c * 2048 + (kc + 1) * 128],
                     pl.xn[:, c * T + n * 512:c * T + (n + 1) * 512],
                     start=(c == 0), stop=(c == NCHUNK - 1))
            p.act(ev, s2ps, AF.Identity, bias=biases[:, kc:kc + 1])
            if cfg["rope"]:
                rope_apply(n)
            src = ev2 if cfg["rope"] else ev
            krow = kc * 128
            p.pdma(ksh[krow:krow + 128, n * 512:(n + 1) * 512],
                   ksh[AH + krow:AH + krow + 128, n * 512:(n + 1) * 512], src, src)
            if mode == "bound":
                stats_from_evac(kssmax, kc)
    if mode == "bound":
        p.pdma(stsh[0:1, :], stsh[1:2, :], kssmax, kssmax)
    shr.set_flag(cm, cfg["flag"])

    load_wcur(wd["qT"], 2048)
    load_bcols(wd["bq"], biases, 16)
    for n in range(4):
        for kc in range(AHC):
            for c in range(NCHUNK):
                p.mm(s2ps, wcur[:, c * 2048 + kc * 128: c * 2048 + (kc + 1) * 128],
                     pl.xn[:, c * T + n * 512:c * T + (n + 1) * 512],
                     start=(c == 0), stop=(c == NCHUNK - 1))
            p.act(ev, s2ps, AF.Identity, bias=biases[:, kc:kc + 1])
            if cfg["rope"]:
                rope_apply(n)
            src = ev2 if cfg["rope"] else ev
            p.dma(qdr[kc * 128:(kc + 1) * 128, n * 512:(n + 1) * 512], src)
            if mode == "bound":
                stats_from_evac(qssmax, kc)

    shr.wait_flag(cm, cfg["flag"])
    negC_ap = None
    if mode == "bound":
        p.pdma(kpart, kpart, stsh[1:2, :], stsh[0:1, :])
        p.tt(kssmax, kssmax, kpart, ALU.max)
        p.tt(tmps, kssmax, qssmax, ALU.mult)
        p.act(tmps2, tmps, AF.Ln, bias=cm.sc(SC_L2EPS))
        p.act(tmps, tmps2, AF.Exp, scale=0.5)
        p.ts(negC, tmps, -1.0, ALU.mult)
        negC_ap = negC

    xattn = pl.xn[:, 0:AHC * T]
    emit_attn_core(p, cm, pl, tag, qdr, xattn, ksh, vsh,
                   nheads=nheads, hd=hd, att_scale=cfg["scale"], causal=causal,
                   mode=mode, negC_bf=negC_ap, amask_sb=amask_sb, pv_bf=pv_bf)

    load_wcur(wd["oT"], 2048)
    load_bcols(wd["bo"], biases, 16)
    xres = pl.f[:, 1024:1536]
    osb2 = pl.f[:, 1536:2048]
    for n in range(4):
        cs = slice(n * 512, (n + 1) * 512)
        for oc in range(NCHUNK):
            for c in range(AHC):
                p.mm(s2ps, wcur[:, c * 2048 + oc * 128: c * 2048 + (oc + 1) * 128],
                     xattn[:, c * T + n * 512: c * T + (n + 1) * 512],
                     start=(c == 0), stop=(c == AHC - 1))
            p.act(osb2, s2ps, AF.Identity, bias=biases[:, oc:oc + 1])
            p.dma(xres, x_d[oc * 128:(oc + 1) * 128, cs])
            p.tt(osb2, osb2, xres, ALU.add)
            p.dma(xout_d[oc * 128:(oc + 1) * 128, cs], osb2)


def rope_tables(parity):
    pos = ((2 * np.arange(16)[:, None] + parity) * 128 + np.arange(128)).reshape(T)
    inv = 1.0 / (10000.0 ** (np.arange(0, 32, 2) / 32.0))
    fr = pos[:, None] * inv[None, :]
    emb = np.concatenate([fr, fr], axis=1)
    cos = np.ones((128, T), np.float32)
    sin = np.zeros((128, T), np.float32)
    cos[0:32, :] = np.cos(emb).T
    sin[0:32, :] = np.sin(emb).T
    R = np.zeros((128, 128), np.float32)
    for m in range(16):
        R[m, 16 + m] = -1.0
        R[16 + m, m] = 1.0
    return cos.astype(NP16), sin.astype(NP16), np.ascontiguousarray(R.T).astype(NP16)


def prep_l0ab(inputs, parity):
    w = {}
    wqkv = _f(inputs["L0_attn_b_w_qkv"]).reshape(D, 16, 3, 128)
    bqkv = _f(inputs["L0_attn_b_b_qkv"]).reshape(16, 3, 128)
    w["qT"] = _f(wqkv[:, :, 0, :].reshape(D, D))
    w["kT"] = _f(wqkv[:, :, 1, :].reshape(D, D))
    w["vT"] = _f(wqkv[:, :, 2, :].reshape(D, D))
    w["bq"] = _f(bqkv[:, 0, :].reshape(D))
    w["bk"] = _f(bqkv[:, 1, :].reshape(D))
    w["bv"] = _f(bqkv[:, 2, :].reshape(D))
    w["oT"] = _f(inputs["L0_attn_b_w_out"])
    w["bo"] = _f(inputs["L0_attn_b_b_out"])
    w["amask"] = build_amask(parity, dtype=BF)
    cos, sin, rT = rope_tables(parity)
    w["costab"], w["sintab"], w["rT"] = cos, sin, rT
    return w


CFG_L0AB = dict(tag="l0ab", nheads=16, hd=128, scale=1.0, causal=True,
                mode="max", flag=Shr.FL_L0AB, rope=True, pv_bf=True)
CFG_L1AA = dict(tag="l1aa", nheads=32, hd=64, scale=1.0 / np.sqrt(32.0),
                causal=False, mode="bound", flag=Shr.FL_L1AA, rope=False,
                pv_bf=True)


def prep_l1aa(inputs, parity):
    w = {}
    w["qT"] = _f(inputs["L1_attn_a_Wq"])
    w["kT"] = _f(inputs["L1_attn_a_Wk"])
    w["vT"] = _f(inputs["L1_attn_a_Wv"])
    w["oT"] = _f(inputs["L1_attn_a_Wo"])
    w["bq"] = _f(inputs["L1_attn_a_bq"])
    w["bk"] = _f(inputs["L1_attn_a_bk"])
    w["bv"] = _f(inputs["L1_attn_a_bv"])
    w["bo"] = _f(inputs["L1_attn_a_bo"])
    return w



def _declare(p, prefix, arrays):
    out = {}
    for k, v in arrays.items():
        if isinstance(v, list):
            out[k] = [p.inp(f"{prefix}{k}{i}", list(a.shape),
                            LP16 if a.dtype == np.float16 else F32)
                      for i, a in enumerate(v)]
        else:
            out[k] = p.inp(f"{prefix}{k}", list(v.shape),
                           LP16 if v.dtype == np.float16 else F32)
    return out


def _feed(im, prefix, arrays):
    for k, v in arrays.items():
        if isinstance(v, list):
            for i, a in enumerate(v):
                im[f"{prefix}{k}{i}"] = a
        else:
            im[f"{prefix}{k}"] = v


def build_launchA(shapes_only_preps):
    p = Prog()
    cm = Common(p)
    x0 = p.inp("x", [D, T], F32)
    x5o = p.out("x5o", [D, T], F32)
    dmem_o = p.out("dmem", [NU_F, 2048], F32)
    dnorm_o = p.out("dnorm", [NU_F, 16], F32)
    wds = {s: _declare(p, s + "_", arrs) for s, arrs in shapes_only_preps.items()}
    pl = Pools(p)
    shr = Shr(p)
    mk = p.out if DEBUG_STREAMS else (lambda n, s, d: p.dram(n, s, d))
    x1 = mk("x1", [D, T], F32)
    x2 = mk("x2", [D, T], F32)
    x3 = mk("x3", [D, T], F32)
    x4 = mk("x4", [D, T], F32)
    x5i = mk("x5i", [D, T], F32)
    xn_dram = p.dram("xnd", [D, T], LP16)

    shr.hello(cm)
    emit_layernorm(p, cm, pl, x0)
    emit_l0_attn_a(p, cm, pl, shr, x0, x1, wds["l0aa"])
    emit_layernorm(p, cm, pl, x1)
    emit_l0ffn(p, cm, pl, x1, x2, wds["l0ffn"])
    emit_layernorm(p, cm, pl, x2)
    emit_attn_std(p, cm, pl, shr, x2, x3, wds["l0ab"], CFG_L0AB)
    emit_layernorm(p, cm, pl, x3)
    emit_l0ffn(p, cm, pl, x3, x4, wds["l0ffn"])
    emit_layernorm(p, cm, pl, x4)
    emit_attn_std(p, cm, pl, shr, x4, x5i, wds["l1aa"], CFG_L1AA)
    emit_layernorm(p, cm, pl, x5i, xn_dram=xn_dram)
    emit_l1ffn(p, cm, pl, x5i, xn_dram, x5o, wds["l1ffn"], "l1f")
    emit_layernorm(p, cm, pl, x5o)
    emit_l1ab_p1(p, cm, pl, x5o, wds["l1ab"], dmem_o, dnorm_o)
    return p.finalize(), len(p.ops)


def build_launchB(shapes_only_preps):
    p = Prog()
    cm = Common(p)
    x5 = p.inp("x5", [D, T], F32)
    y = p.out("y", [D, T], F32)
    parts = {"dmem": [p.inp(f"dmem{r}", [NU_F, 2048], F32) for r in range(8)],
             "dnorm": [p.inp(f"dnorm{r}", [NU_F, 16], F32) for r in range(8)]}
    wds = {s: _declare(p, s + "_", arrs) for s, arrs in shapes_only_preps.items()}
    pl = Pools(p)
    x6 = p.dram("x6", [D, T], F32)
    xn_dram = p.dram("xnd", [D, T], LP16)

    emit_layernorm(p, cm, pl, x5)
    emit_l1ab_p2(p, cm, pl, x5, x6, wds["l1ab"], parts)
    emit_layernorm(p, cm, pl, x6, xn_dram=xn_dram)
    emit_l1ffn(p, cm, pl, x6, xn_dram, y, wds["l1ffn"], "l1f")
    return p.finalize(), len(p.ops)


def kernel(**inputs):
    import os as _os
    consts, crow = host_consts()
    identb, identf = host_ident()
    nonce = np.array([[int.from_bytes(_os.urandom(4), "little") & 0x3FFFFFFF | 1,
                       0, 0, 0]], np.int32)

    preps_par = [
        {"l0aa": prep_l0aa(inputs, par), "l0ffn": prep_l0ffn(inputs),
         "l0ab": prep_l0ab(inputs, par), "l1aa": prep_l1aa(inputs, par),
         "l1ffn": prep_l1ffn(inputs), "l1ab": prep_l1ab(inputs)}
        for par in (0, 1)
    ]

    ncA, nopsA = build_launchA(preps_par[0])
    shards = []
    for b in range(4):
        ev, od = shard_tokens(np.asarray(inputs["L0_attn_a_x"][b]))
        shards.extend([ev, od])
    in_mapsA = []
    for core in range(8):
        par = core & 1
        im = dict(x=shards[core], consts=consts, crow=crow, ident_bf=identb,
                  ident_f32=identf, nonce=nonce)
        for s, arrs in preps_par[par].items():
            _feed(im, s + "_", arrs)
        in_mapsA.append(im)
    resA = run_bass_kernel_spmd(ncA, in_mapsA, core_ids=list(range(8)))

    ncB, nopsB = build_launchB({k: preps_par[0][k] for k in ("l1ffn", "l1ab")})
    in_mapsB = []
    for core in range(8):
        par = core & 1
        im = dict(x5=np.asarray(resA.results[core]["x5o"]), consts=consts,
                  crow=crow, ident_bf=identb, ident_f32=identf, nonce=nonce)
        for r in range(8):
            im[f"dmem{r}"] = np.asarray(resA.results[r]["dmem"])
            im[f"dnorm{r}"] = np.asarray(resA.results[r]["dnorm"])
        for s in ("l1ffn", "l1ab"):
            _feed(im, s + "_", preps_par[par][s])
        in_mapsB.append(im)
    resB = run_bass_kernel_spmd(ncB, in_mapsB, core_ids=list(range(8)))

    out = np.empty((4, 4096, D), np.float32)
    for b in range(4):
        out[b] = unshard_tokens(resB.results[2 * b]["y"], resB.results[2 * b + 1]["y"])
    return out


def emit_l1ffn(p: Prog, cm: Common, pl: Pools, x_d, xn_d, xout_d, wd, tag):
    wcur = pl.w[:, 0:32768]
    xnsl = pl.xn[:, 16384:24576]
    xnv = xnsl.rearrange("p (c t) -> p c t", t=512)
    racc = pl.xn[:, 0:16384].bitcast(F32)
    cw_f = pl.rows[0:9, 0:2048]
    esel = pl.rows[0:9, 2048:3200]
    hps = pl.pp0[:, 0:512]
    dps = pl.pp0[:, 512:1024]
    cwx = pl.pp1[:, 0:512]
    scps = pl.pp1[:, 1024:1536]
    trps = pl.pp1[0:9, 512:1024]
    gse = pl.ab[:, 15360:15504]
    m1 = pl.f[:, 2048:2049]
    nm1 = pl.f[:, 2049:2050]
    m2 = pl.f[:, 2050:2051]
    ll = pl.f[:, 2051:2052]
    rl = pl.f[:, 2052:2053]
    ex = pl.f[:, 2056:2064]
    msk = pl.f[:, 2064:2072]
    s2 = pl.f[:, 2072:2080]
    ww = pl.f[:, 2080:2088]
    comb9 = pl.f[:, 2088:2097]
    xres = pl.f[:, 1024:1536]
    osb = pl.f[:, 1536:2048]

    p.dma(esel, wd["esel"])
    for c in range(NCHUNK):
        load_w(p, wd["gwseT"], gse[:, c * 9:(c + 1) * 9], c, slice(0, 9))

    for n in range(4):
        p.dma(xnsl, xn_d.rearrange("(c p) t -> p c t", p=128)[:, :, n * 512:(n + 1) * 512])
        for tc4 in range(4):
            tc = n * 4 + tc4
            for c in range(NCHUNK):
                p.mm(scps[:, 0:9], xnv[:, c, tc4 * 128:(tc4 + 1) * 128],
                     gse[:, c * 9:(c + 1) * 9], start=(c == 0), stop=(c == NCHUNK - 1))
            sc8 = scps[:, 0:8]
            p.red(m1, sc8, ALU.max)
            p.ts(nm1, m1, -1.0, ALU.mult)
            p.act(ex, sc8, AF.Exp, bias=nm1)
            p.ts(msk, sc8, m1, ALU.is_ge)
            p.ts(s2, msk, -1e30, ALU.mult)
            p.tt(s2, sc8, s2, ALU.add)
            p.red(m2, s2, ALU.max)
            p.ts(msk, sc8, m2, ALU.is_ge)
            p.tt(ww, ex, msk, ALU.mult)
            p.red(ll, ww, ALU.add)
            p.recip(rl, ll)
            p.ts(comb9[:, 0:8], ww, rl, ALU.mult)
            p.act(comb9[:, 8:9], scps[:, 8:9], AF.Sigmoid)
            p.mm(trps[0:9, 0:128], comb9, cm.identf[:], start=True, stop=True)
            p.copy(cw_f[:, tc * 128:(tc + 1) * 128], trps[0:9, 0:128])

    for n in range(4):
        p.dma(xnsl, xn_d.rearrange("(c p) t -> p c t", p=128)[:, :, n * 512:(n + 1) * 512])
        for half, wkey, dst in ((0, "shgT", pl.ab[:, 0:8192]), (1, "shuT", pl.ab[:, 8192:16384])):
            for c in range(NCHUNK):
                load_w(p, wd[wkey], wcur[:, c * 2048:(c + 1) * 2048], c, slice(0, 2048))
            for hc in range(16):
                for c in range(NCHUNK):
                    p.mm(hps, wcur[:, c * 2048 + hc * 128: c * 2048 + (hc + 1) * 128],
                         xnv[:, c, :], start=(c == 0), stop=(c == NCHUNK - 1))
                p.act(dst[:, hc * 512:(hc + 1) * 512], hps,
                      AF.Silu if half == 0 else AF.Identity)
        h1, h3 = pl.ab[:, 0:8192], pl.ab[:, 8192:16384]
        if f"dbg_pre{n}" in wd:
            p.dma(wd[f"dbg_pre{n}"], h1)
            p.dma(wd[f"dbg_xn{n}"], h3)
        p.mm(cwx, esel[0:9, 8 * 128:9 * 128], cw_f[0:9, n * 512:(n + 1) * 512],
             start=True, stop=True)
        for hc in range(16):
            hsl = slice(hc * 512, (hc + 1) * 512)
            p.tt(h1[:, hsl], h1[:, hsl], h3[:, hsl], ALU.mult)
            p.tt(h1[:, hsl], h1[:, hsl], cwx, ALU.mult)
        if f"dbg_hh{n}" in wd:
            p.dma(wd[f"dbg_hh{n}"], h1)
            p.copy(pl.f[:, 1536:2048], cwx)
            p.dma(wd[f"dbg_cwx{n}"], pl.f[:, 1536:2048])
        for c in range(NCHUNK):
            load_w(p, wd["shdT"], wcur[:, c * 2048:(c + 1) * 2048], c, slice(0, 2048))
        for oc in range(16):
            for c in range(NCHUNK):
                p.mm(dps, wcur[:, c * 2048 + oc * 128: c * 2048 + (oc + 1) * 128],
                     h1[:, c * 512:(c + 1) * 512], start=(c == 0), stop=(c == NCHUNK - 1))
            p.copy(racc[:, oc * 512:(oc + 1) * 512], dps)
        if f"dbg_sh{n}" in wd:
            p.dma(wd[f"dbg_sh{n}"], racc)
            p.dma(wd[f"dbg_h1{n}"], h1)
        h1e, h3e = pl.ab[:, 0:4096], pl.ab[:, 4096:8192]
        for e in range(8):
            for half, wkey, dst, fn in ((0, "wgT", h1e, AF.Silu), (1, "wuT", h3e, AF.Identity)):
                wde = wd[wkey][e]
                for c in range(NCHUNK):
                    load_w(p, wde, wcur[:, c * 1024:(c + 1) * 1024], c, slice(0, 1024))
                for hc in range(8):
                    for c in range(NCHUNK):
                        p.mm(hps, wcur[:, c * 1024 + hc * 128: c * 1024 + (hc + 1) * 128],
                             xnv[:, c, :], start=(c == 0), stop=(c == NCHUNK - 1))
                    p.act(dst[:, hc * 512:(hc + 1) * 512], hps, fn)
            p.mm(cwx, esel[0:9, e * 128:(e + 1) * 128], cw_f[0:9, n * 512:(n + 1) * 512],
                 start=True, stop=True)
            for hc in range(8):
                hsl = slice(hc * 512, (hc + 1) * 512)
                p.tt(h1e[:, hsl], h1e[:, hsl], h3e[:, hsl], ALU.mult)
                p.tt(h1e[:, hsl], h1e[:, hsl], cwx, ALU.mult)
            wde = wd["wdT"][e]
            for c in range(8):
                load_w(p, wde, wcur[:, 16384 + c * 2048:16384 + (c + 1) * 2048], c, slice(0, 2048))
            for oc in range(16):
                for c in range(8):
                    p.mm(dps, wcur[:, 16384 + c * 2048 + oc * 128: 16384 + c * 2048 + (oc + 1) * 128],
                         h1e[:, c * 512:(c + 1) * 512], start=(c == 0), stop=(c == 7))
                p.tt(racc[:, oc * 512:(oc + 1) * 512], racc[:, oc * 512:(oc + 1) * 512],
                     dps, ALU.add)
        cs = slice(n * 512, (n + 1) * 512)
        for oc in range(16):
            p.dma(xres, x_d[oc * 128:(oc + 1) * 128, cs])
            p.tt(osb, racc[:, oc * 512:(oc + 1) * 512], xres, ALU.add)
            p.dma(xout_d[oc * 128:(oc + 1) * 128, cs], osb)


def prep_l1ffn(inputs):
    w = {}
    gw = _f(inputs["L1_ffn_gate_w"])
    se = _f(inputs["L1_ffn_se_gate_w"])
    w["gwseT"] = _f(np.concatenate([gw, se], axis=0).T)
    w["wgT"] = [_f(inputs["L1_ffn_w_gate"][e].T) for e in range(8)]
    w["wuT"] = [_f(inputs["L1_ffn_w_up"][e].T) for e in range(8)]
    w["wdT"] = [_f(inputs["L1_ffn_w_down"][e].T) for e in range(8)]
    w["shgT"] = _f(np.asarray(inputs["L1_ffn_sh_gate"]).T)
    w["shuT"] = _f(np.asarray(inputs["L1_ffn_sh_up"]).T)
    w["shdT"] = _f(np.asarray(inputs["L1_ffn_sh_down"]).T)
    esel = np.zeros((9, 9 * 128), np.float32)
    for e in range(9):
        esel[e, e * 128:(e + 1) * 128] = 1.0
    w["esel"] = esel
    return w

NU_F = 768


def emit_l1ab_p1(p: Prog, cm: Common, pl: Pools, x_d, wd, dmem_o, dnorm_o):
    tag = "l1abp1"
    ksb = pl.ab[:, 0:8192]
    vsb = pl.ab[:, 8192:16384]
    wmv = pl.xn
    mbdr = p.dram(tag + "_mb", [T, 2048], LP16)
    mkdr = p.dram(tag + "_mk", [T, 4 * NU_F], LP16)
    s2ps = pl.pp0[:, 512:1024]
    ndps = pl.pp0[:, 0:512]
    tpsf = pl.pp1[:, 512:1024].bitcast(LP16)
    dpsA = pl.pp1[:, 0:512]
    dpsB = pl.pp0[:, 1024:1536]
    brow = pl.w[0:1, 32768:34816]
    biases = pl.f[:, 2100:2116]

    xq = pl.w[:, 0:256]
    mkraw = pl.w[:, 256:1024]
    mkfm = pl.w[:, 1024:1792]
    pvsb = pl.w[:, 1792:1920]
    mbsb = pl.w[:, 1920:2048]
    mvsb = pl.w[:, 2048:2176]
    memnorm = pl.w[:, 2176:14560]
    sstmp = pl.f[:, 512:513]
    rn = pl.f[:, 513:514]
    msq = pl.f[:, 514:515]
    rmsq = pl.f[:, 515:516]
    dcol = pl.f[:, 516:517]
    rd = pl.f[:, 517:518]
    t2c = pl.f[:, 518:519]
    coefs = pl.w[:, 14560:14816]
    def load_bcols(b_d, dst, n):
        bt = b_d.rearrange("(c r) -> c r", r=128)
        for c in range(n):
            p.dma(dst[:, c:c + 1], bt[c:c + 1, :].rearrange("a r -> r a"))

    for c in range(NCHUNK):
        load_w(p, wd["wkT"], pl.w[:, c * 512:(c + 1) * 512], c, slice(0, 512))
        load_w(p, wd["wvT"], pl.w[:, 8192 + c * 512:8192 + (c + 1) * 512], c, slice(0, 512))
    p.dma_cast(brow[0:1, 0:512], wd["bk"].unsqueeze(0))
    p.dma_cast(brow[0:1, 512:1024], wd["bv"].unsqueeze(0))
    for tc in range(NTCH):
        for (base, dst, boff) in ((0, ksb, 0), (8192, vsb, 512)):
            for c in range(NCHUNK):
                p.mm(s2ps, pl.xn[:, c * T + tc * 128:c * T + (tc + 1) * 128],
                     pl.w[:, base + c * 512:base + (c + 1) * 512],
                     start=(c == 0), stop=False)
            p.mm(s2ps, cm.onesb[:], brow[0:1, boff:boff + 512], start=False, stop=True)
            p.copy(dst[:, tc * 512:(tc + 1) * 512], s2ps, eng="scalar")
    for c in range(NCHUNK):
        load_w(p, wd["wmbT"], pl.w[:, c * 2048:(c + 1) * 2048], c, slice(0, 2048))
    p.dma_cast(brow[0:1, 0:2048], wd["bmb"].unsqueeze(0))
    evmb = pl.f[:, 1536:2048].bitcast(LP16)
    for tc in range(NTCH):
        for q4 in range(4):
            ocs = slice(q4 * 512, (q4 + 1) * 512)
            for c in range(NCHUNK):
                p.mm(s2ps, pl.xn[:, c * T + tc * 128:c * T + (tc + 1) * 128],
                     pl.w[:, c * 2048 + q4 * 512:c * 2048 + (q4 + 1) * 512],
                     start=(c == 0), stop=False)
            p.mm(s2ps, cm.onesb[:], brow[0:1, ocs], start=False, stop=True)
            p.act(evmb[:, 0:512], s2ps, AF.Sigmoid)
            p.dma(mbdr[tc * 128:(tc + 1) * 128, ocs], evmb[:, 0:512])

    p.dma_cast(memnorm.rearrange("p (a c) -> p a c", c=2064),
               wd["memnorm"].rearrange("(a p) c -> p a c", p=128))

    for tc in range(NTCH):
        for c in range(4):
            ksl = ksb[:, tc * 512 + c * 128: tc * 512 + (c + 1) * 128]
            p.act(xq[:, 0:128], ksl, AF.Relu)
            p.act(xq[:, 128:256], ksl, AF.Relu, scale=-1.0)
            for r in range(3):
                w1 = 255 - r
                p.tt(mkraw[:, 256 * r + r + 1: 256 * (r + 1)], xq[:, r + 1:256],
                     xq[:, 0:w1], ALU.mult)
                p.tt(mkraw[:, 256 * r: 256 * r + r + 1], xq[:, 0:r + 1],
                     xq[:, 255 - r:256], ALU.mult)
            sqm = pl.f[:, 520:904].bitcast(LP16)
            p.tt(sqm, mkraw, mkraw, ALU.mult)
            p.red(sstmp, sqm, ALU.add)
            p.copy(msq, sstmp)
            p.act(t2c, sstmp, AF.Ln, bias=1e-24)
            p.act(rn, t2c, AF.Exp, scale=-0.5)
            p.ts(mkraw, mkraw, rn, ALU.mult)
            p.ts(t2c, rn, rn, ALU.mult)
            p.tt(msq, msq, t2c, ALU.mult)
            p.recip(rmsq, msq)
            p.dma(mkdr[tc * 128:(tc + 1) * 128, c * NU_F:(c + 1) * NU_F], mkraw)
            for fc in range(6):
                p.transp(tpsf[:, 0:128], mkraw[:, fc * 128:(fc + 1) * 128], cm.ident[:])
                p.copy(mkfm[:, fc * 128:(fc + 1) * 128], tpsf[:, 0:128], eng="scalar")
            for h in range(4 * c, 4 * c + 4):
                for fc in range(6):
                    p.mm(ndps[:, 0:129], mkfm[:, fc * 128:(fc + 1) * 128],
                         memnorm[:, (fc * 16 + h) * 129:(fc * 16 + h + 1) * 129],
                         start=(fc == 0), stop=(fc == 5))
                p.ts(dcol, ndps[:, 128:129], 1e-8, ALU.add)
                p.recip(rd, dcol)
                p.ts(pvsb, ndps[:, 0:128], rd, ALU.mult)
                p.dma(mbsb, mbdr[tc * 128:(tc + 1) * 128, h * 128:(h + 1) * 128])
                vsl = vsb[:, tc * 512 + c * 128: tc * 512 + (c + 1) * 128]
                p.tt(mvsb, vsl, pvsb, ALU.subtract)
                p.tt(wmv[:, tc * 2048 + h * 128: tc * 2048 + (h + 1) * 128],
                     mvsb, mbsb, ALU.mult)
                p.ts(t2c, dcol, rmsq, ALU.mult)
                p.ts(t2c, t2c, -1.0, ALU.mult, 1.0, ALU.add)
                p.ts(coefs[:, tc * 16 + h: tc * 16 + h + 1], t2c, 0.0, ALU.max, 1.0, ALU.min)

    mkt = pl.w[:, 0:128]
    wvt = pl.xn
    evd = pl.f[:, 1536:2048]
    for c in range(4):
        for fc in range(6):
            for tc in range(NTCH):
                p.dma(mkt, mkdr[tc * 128:(tc + 1) * 128, c * NU_F + fc * 128: c * NU_F + (fc + 1) * 128])
                p.mm(dpsA, mkt, wmv[:, tc * 2048 + 4 * c * 128: tc * 2048 + (4 * c + 4) * 128],
                     start=(tc == 0), stop=(tc == NTCH - 1))
                p.mm(dpsB[:, 0:4], mkt,
                     coefs[:, tc * 16 + 4 * c: tc * 16 + 4 * c + 4],
                     start=(tc == 0), stop=(tc == NTCH - 1))
            p.copy(evd, dpsA)
            p.dma(dmem_o[fc * 128:(fc + 1) * 128, 4 * c * 128:(4 * c + 4) * 128], evd)
            p.copy(evd[:, 0:4], dpsB[:, 0:4])
            p.dma(dnorm_o[fc * 128:(fc + 1) * 128, 4 * c:4 * c + 4], evd[:, 0:4])


def prep_l1ab(inputs):
    w = {}
    w["wkT"] = _f(np.asarray(inputs["L1_attn_b_Wk"]).T)
    w["wvT"] = _f(np.asarray(inputs["L1_attn_b_Wv"]).T)
    w["wmbT"] = _f(np.asarray(inputs["L1_attn_b_Wmb"]).T)
    w["wqT"] = _f(np.asarray(inputs["L1_attn_b_Wq"]).T)
    w["wgT"] = _f(np.asarray(inputs["L1_attn_b_Wg"]).T)
    w["bk"] = _f(inputs["L1_attn_b_bk"])
    w["bv"] = _f(inputs["L1_attn_b_bv"])
    w["bmb"] = _f(inputs["L1_attn_b_bmb"])
    w["bq"] = _f(inputs["L1_attn_b_bq"])
    w["bg"] = _f(inputs["L1_attn_b_bg"])
    mem = _f(inputs["L1_attn_b_memory"])
    nrm = _f(inputs["L1_attn_b_norm"])
    mn = np.zeros((NU_F, 16 * 129), np.float32)
    for h in range(16):
        mn[:, h * 129:h * 129 + 128] = mem[h]
        mn[:, h * 129 + 128] = nrm[h]
    w["memnorm"] = _f(mn)
    return w

def emit_l1ab_p2(p: Prog, cm: Common, pl: Pools, x_d, xout_d, wd, parts):
    tag = "l1abp2"
    qdr = p.dram(tag + "_q", [T, 2048], LP16)
    s2ps = pl.pp0[:, 512:1024]
    ndps = pl.pp0[:, 0:512]
    tpsf = pl.pp1[:, 512:1024].bitcast(LP16)
    brow2 = pl.ab[0:1, 0:2048]
    biases = pl.f[:, 2100:2116]

    def load_bcols(b_d, dst, n):
        bt = b_d.rearrange("(c r) -> c r", r=128)
        for c in range(n):
            p.dma(dst[:, c:c + 1], bt[c:c + 1, :].rearrange("a r -> r a"))

    for c in range(NCHUNK):
        load_w(p, wd["wqT"], pl.w[:, c * 2048:(c + 1) * 2048], c, slice(0, 2048))
    p.dma_cast(brow2, wd["bq"].unsqueeze(0))
    for tc in range(NTCH):
        for q4 in range(4):
            ocs = slice(q4 * 512, (q4 + 1) * 512)
            for c in range(NCHUNK):
                p.mm(s2ps, pl.xn[:, c * T + tc * 128:c * T + (tc + 1) * 128],
                     pl.w[:, c * 2048 + q4 * 512:c * 2048 + (q4 + 1) * 512],
                     start=(c == 0), stop=False)
            p.mm(s2ps, cm.onesb[:], brow2[0:1, ocs], start=False, stop=True)
            p.copy(pl.f[:, 0:512].bitcast(LP16)[:, 0:512], s2ps, eng="scalar")
            p.dma(qdr[tc * 128:(tc + 1) * 128, ocs], pl.f[:, 0:512].bitcast(LP16)[:, 0:512])

    nm = pl.w[:, 0:12384]
    accm = pl.rows[:, 0:2048]
    prtb = pl.rows[:, 2048:4096]
    accn = pl.f[:, 0:16]
    prtn = pl.f[:, 16:32]
    mview = wd["memnorm"].rearrange("f (h c) -> f h c", c=129)
    for fc in range(6):
        rows = slice(fc * 128, (fc + 1) * 128)
        p.dma(accm.rearrange("p (h c) -> p h c", c=128), mview[rows, :, 0:128])
        p.dma(accn.rearrange("p (h c) -> p h c", c=1), mview[rows, :, 128:129],
              allow_slow=True)
        for r in range(len(parts["dmem"])):
            p.dma(prtb, parts["dmem"][r][rows, :])
            p.tt(accm, accm, prtb, ALU.add)
            p.dma(prtn, parts["dnorm"][r][rows, :])
            p.tt(accn, accn, prtn, ALU.add)
        for h in range(16):
            p.copy(nm[:, (fc * 16 + h) * 129:(fc * 16 + h) * 129 + 128],
                   accm[:, h * 128:(h + 1) * 128])
            p.copy(nm[:, (fc * 16 + h) * 129 + 128:(fc * 16 + h) * 129 + 129],
                   accn[:, h:h + 1])

    xq = pl.w[:, 12384:12640]
    mkraw = pl.w[:, 12640:13408]
    mkfm = pl.w[:, 13408:14176]
    w1t = pl.w[:, 14176:16224]
    qsl = pl.w[:, 16224:16352]
    mhs = pl.w[:, 16352:16480]
    mh_fm = pl.ab[:, 0:8192]
    gsb = pl.ab[:, 8192:8704]
    sqm = pl.f[:, 520:904].bitcast(LP16)
    sstmp = pl.f[:, 512:513]
    rn = pl.f[:, 513:514]
    t2c = pl.f[:, 514:515]
    dcol = pl.f[:, 516:517]
    rd = pl.f[:, 517:518]
    t1 = pl.f[:, 1024:1536]
    xres = pl.f[:, 1536:2048]
    load_bcols(wd["bg"], biases, 16)

    for n in range(4):
        for tcw in range(4):
            tc = 4 * n + tcw
            for h in range(16):
                p.dma(qsl, qdr[tc * 128:(tc + 1) * 128, h * 128:(h + 1) * 128])
                p.act(xq[:, 0:128], qsl, AF.Relu)
                p.act(xq[:, 128:256], qsl, AF.Relu, scale=-1.0)
                for r in range(3):
                    w1 = 255 - r
                    p.tt(mkraw[:, 256 * r + r + 1: 256 * (r + 1)], xq[:, r + 1:256],
                         xq[:, 0:w1], ALU.mult)
                    p.tt(mkraw[:, 256 * r: 256 * r + r + 1], xq[:, 0:r + 1],
                         xq[:, 255 - r:256], ALU.mult)
                p.tt(sqm, mkraw, mkraw, ALU.mult)
                p.red(sstmp, sqm, ALU.add)
                p.act(t2c, sstmp, AF.Ln, bias=1e-24)
                p.act(rn, t2c, AF.Exp, scale=-0.5)
                p.ts(mkraw, mkraw, rn, ALU.mult)
                for fc in range(6):
                    p.transp(tpsf[:, 0:128], mkraw[:, fc * 128:(fc + 1) * 128], cm.ident[:])
                    p.copy(mkfm[:, fc * 128:(fc + 1) * 128], tpsf[:, 0:128], eng="scalar")
                for fc in range(6):
                    p.mm(ndps[:, 0:129], mkfm[:, fc * 128:(fc + 1) * 128],
                         nm[:, (fc * 16 + h) * 129:(fc * 16 + h + 1) * 129],
                         start=(fc == 0), stop=(fc == 5))
                p.ts(dcol, ndps[:, 128:129], 1e-8, ALU.add)
                p.recip(rd, dcol)
                p.ts(mhs, ndps[:, 0:128], rd, ALU.mult)
                p.transp(tpsf[:, 0:128], mhs, cm.ident[:])
                p.copy(mh_fm[:, h * 512 + tcw * 128: h * 512 + (tcw + 1) * 128],
                       tpsf[:, 0:128], eng="scalar")
        cs = slice(n * 512, (n + 1) * 512)
        for oc in range(NCHUNK):
            p.dma_cast(w1t.rearrange("p (c o) -> p c o", o=128),
                       wd["wgT"].rearrange("(c r) o -> r c o", r=128)[:, :, oc * 128:(oc + 1) * 128])
            for c in range(NCHUNK):
                p.mm(s2ps, w1t[:, c * 128:(c + 1) * 128],
                     pl.xn[:, c * T + n * 512:c * T + (n + 1) * 512],
                     start=(c == 0), stop=(c == NCHUNK - 1))
            p.act(gsb, s2ps, AF.Sigmoid, bias=biases[:, oc:oc + 1])
            p.tt(t1, gsb, mh_fm[:, oc * 512:(oc + 1) * 512], ALU.mult)
            p.tt(t1, t1, pl.xn[:, oc * T + n * 512:oc * T + (n + 1) * 512], ALU.add)
            p.dma(xres, x_d[oc * 128:(oc + 1) * 128, cs])
            p.tt(t1, t1, xres, ALU.add)
            p.dma(xout_d[oc * 128:(oc + 1) * 128, cs], t1)
